# version 15
# baseline (speedup 1.0000x reference)
import numpy as np
import ml_dtypes
import concourse.bass as bass
import concourse.mybir as mybir
from concourse.bass_utils import run_bass_kernel_spmd
from concourse.alu_op_type import AluOpType
from contextlib import ExitStack

F32 = mybir.dt.float32
BF16 = mybir.dt.bfloat16
I32 = mybir.dt.int32
AF = mybir.ActivationFunctionType
AX = mybir.AxisListType

H = 16
DK = 128
M = 64
D = 2048
T = 2048
NB = 2
NCC = D // 128
NTT = T // 128
INTER = 5632
NIC = INTER // 128
ICB = 22
EPS_LN = 1e-5
EPS_RMS = 1e-5
CLAMP = 32.0
SCALE = float(DK) ** -0.5
ROPE_BASE = 10000.0
PI = float(np.pi)


class TK:
    ENGINES = ("pe", "dve", "act", "pool", "sp")
    Q2E = {"qsp": "sp", "qpool": "pool", "qact": "act", "qdve": "dve"}
    N_DMA_SEMS = 48

    def __init__(self, nc, stack):
        self.nc = nc
        self.sem = {}
        self.cnt = {}
        self.regions = {}
        self.actions = {e: [] for e in self.ENGINES}
        self.know = {e: {} for e in self.ENGINES}
        self.know_at = {}
        for s in self.ENGINES:
            self.sem[s] = stack.enter_context(nc.semaphore(f"sem_{s}"))
            self.cnt[s] = 0
        self.dma_sems = {"sw": [], "hw": []}
        for i in range(self.N_DMA_SEMS):
            cls = "sw" if i < self.N_DMA_SEMS // 2 else "hw"
            s = f"qd{i}"
            self.sem[s] = stack.enter_context(nc.semaphore(s))
            self.cnt[s] = 0
            self.dma_sems[cls].append(s)
        self.dma_rr = {"sw": 0, "hw": 0}

    def _wait(self, e, sem_name, val):
        if val <= 0:
            return
        if sem_name == e:
            if self.cnt[e] < val:
                raise RuntimeError(f"self-wait on future: {e} {val} > {self.cnt[e]}")
            return
        kn = self.know[e]
        if kn.get(sem_name, 0) >= val:
            return
        self.actions[e].append(("wait", sem_name, val))
        kn[sem_name] = val
        prod = self.know_at.get((sem_name, val))
        if prod:
            for s2, v2 in prod.items():
                if kn.get(s2, 0) < v2:
                    kn[s2] = v2

    def _wait_self(self, e, val):
        kn = self.know[e]
        if kn.get(e, 0) >= val:
            return
        self.actions[e].append(("wait", e, val))
        kn[e] = val
        prod = self.know_at.get((e, val))
        if prod:
            for s2, v2 in prod.items():
                if kn.get(s2, 0) < v2:
                    kn[s2] = v2

    def _collect(self, reads, writes):
        need = {}
        for k in reads:
            r = self.regions.get(k)
            if r and r["w"]:
                s, c = r["w"]
                need[s] = max(need.get(s, 0), c)
        for k in writes:
            r = self.regions.get(k)
            if r:
                if r["w"]:
                    s, c = r["w"]
                    need[s] = max(need.get(s, 0), c)
                for s, c in r["r"].items():
                    need[s] = max(need.get(s, 0), c)
        return need

    def _update(self, reads, writes, tag):
        s, c = tag
        for k in writes:
            self.regions[k] = {"w": tag, "r": {}}
        for k in reads:
            r = self.regions.setdefault(k, {"w": None, "r": {}})
            r["r"][s] = max(r["r"].get(s, 0), c)

    def _emit_waits(self, e, need):
        for s, c in sorted(need.items()):
            if s == e:
                if e != "pe":
                    self._wait_self(e, c)
            else:
                self._wait(e, s, c)

    def op(self, e, build, reads=(), writes=()):
        need = self._collect(reads, writes)
        self._emit_waits(e, need)
        self.cnt[e] += 1
        tag = (e, self.cnt[e])
        kn = dict(self.know[e])
        kn[e] = self.cnt[e]
        self.know_at[tag] = kn
        self.actions[e].append(("inst", build, e, 1, self.cnt[e]))
        self._update(reads, writes, tag)

    def dma(self, q, dst, src, reads=(), writes=(), **kw):
        e = self.Q2E[q]
        cls = "sw" if q == "qpool" else "hw"
        need = self._collect(reads, writes)
        pool = self.dma_sems[cls]
        s = pool[self.dma_rr[cls]]
        self.dma_rr[cls] = (self.dma_rr[cls] + 1) % len(pool)
        if self.cnt[s] > 0:
            need[s] = max(need.get(s, 0), self.cnt[s])
        self._emit_waits(e, need)
        self.cnt[s] += 16
        tag = (s, self.cnt[s])
        kn = dict(self.know[e])
        kn[s] = self.cnt[s]
        self.know_at[tag] = kn
        self.actions[e].append(("inst", lambda eng: eng.dma_start(dst, src, **kw),
                                s, 16, self.cnt[s]))
        self._update(reads, writes, tag)

    def fence_all(self, e="sp"):
        need = {s: c for s, c in self.cnt.items() if c > 0 and s != e}
        self._emit_waits(e, need)

    def fence(self):
        for e in self.ENGINES:
            need = {s: c for s, c in self.cnt.items() if c > 0 and s != e}
            self._emit_waits(e, need)

    def finalize_incs(self):
        needed = {}
        for e in self.ENGINES:
            for a in self.actions[e]:
                if a[0] == "wait":
                    needed.setdefault(a[1], set()).add(a[2])
        self.remap = {}
        for s in self.sem:
            vals = sorted(needed.get(s, ()))
            self.remap[s] = vals

    def _new_val(self, s, old):
        import bisect
        vals = self.remap[s]
        idx = bisect.bisect_right(vals, old)
        return idx

    def replay(self, e, eng):
        step = {s: (16 if s.startswith("qd") else 1) for s in self.sem}
        for a in self.actions[e]:
            if a[0] == "wait":
                _, s, old = a
                eng_newval = self._new_val(s, old) * step[s]
                eng.wait_ge(self.sem[s], eng_newval)
            else:
                _, build, s, amt, old = a
                inst = build(eng)
                if old in self.remap_set(s):
                    inst.then_inc(self.sem[s], amt)

    def remap_set(self, s):
        if not hasattr(self, "_rs"):
            self._rs = {k: set(v) for k, v in self.remap.items()}
        return self._rs.get(s, set())


def build_program(phases=("p0", "attn", "w", "ffn"), dbg=(), nheads=H, nbat=NB,
                  attn_upto=99):
    nc = bass.Bass("TRN2", target_bir_lowering=False)
    dt = {}

    def din(name, shape, dtyp=F32):
        dt[name] = nc.dram_tensor(name, shape, dtyp, kind="ExternalInput").ap()
        return dt[name]

    def dout(name, shape, dtyp=F32):
        dt[name] = nc.dram_tensor(name, shape, dtyp, kind="ExternalOutput").ap()
        return dt[name]

    xT = din("xT", [NB, NCC, 128, T])
    wq = din("wq", [H, 128, NCC * DK])
    wk = din("wk", [H, 128, NCC * DK])
    wv = din("wv", [H, 128, NCC * DK])
    wg = din("wg", [H, 128, NCC * DK])
    wo = din("wo", [H, 128, D])
    ws1t = din("ws1t", [128, NCC * 16])
    ws2b = din("ws2b", [17, H * M])
    gnw_in = din("gnw", [128, 1])
    gate = din("gate", [NIC, 128, NCC * 128])
    upw = din("upw", [NIC, 128, NCC * 128])
    down = din("down", [NCC, 128, NIC * 128])
    gs_in = din("gs", [128, NIC])
    us_in = din("us", [128, NIC])
    ds_in = din("ds", [128, NCC])
    outT = dout("outT", [NB, NCC, 128, T])
    ogt_s = nc.dram_tensor("ogt_s", [128, H, T], BF16).ap()
    part_s = nc.dram_tensor("part_s", [NCC, 128, T], F32).ap()

    with ExitStack() as st:
        def sb(name, shape, dtyp=F32):
            return st.enter_context(nc.sbuf_tensor(name, shape, dtyp))

        def psm(name, shape, dtyp=F32):
            return st.enter_context(nc.psum_tensor(name, shape, dtyp))

        tk = TK(nc, st)

        xnT = sb("xnT", [128, NCC * T], BF16)
        t1aug = sb("t1aug", [17, T], BF16)
        ws1_sb = sb("ws1_sb", [128, NCC * 16], BF16)
        ws2_sb = sb("ws2_sb", [17, H * M], BF16)
        gnw_sb = sb("gnw_sb", [128, 1], F32)
        gs_sb = sb("gs_sb", [128, NIC], F32)
        us_sb = sb("us_sb", [128, NIC], F32)
        ds_sb = sb("ds_sb", [128, NCC], F32)
        identb = sb("identb", [128, 128], BF16)
        identf64 = sb("identf64", [64, 64], F32)
        triu_m = sb("triu_m", [128, 128], F32)
        onescol = sb("onescol", [128, 1], BF16)
        ones_f = sb("ones_f", [1, 128], F32)
        negones_f = sb("negones_f", [1, 128], F32)
        phalf16 = sb("phalf16", [128, NTT], F32)
        cb32 = sb("cb32", [128, 1], F32)
        onesrow_b = sb("onesrow_b", [1, T], BF16)
        cb64 = sb("cb64", [128, 1], F32)
        ifcol = sb("ifcol", [64, 1], F32)
        cke_sb = [sb(f"cke{i}", [128, M], BF16) for i in range(2)]
        cev_sb = [sb(f"cev{i}", [64, 128], BF16) for i in range(2)]
        on_sb = [sb(f"on{i}", [128, 128], BF16) for i in range(4)]
        negmax_sb = sb("negmax_sb", [128, NTT], F32)
        rowsum_sb = sb("rowsum_sb", [128, NTT], F32)
        rcp16_sb = sb("rcp16_sb", [128, NTT], F32)
        ss_sb = sb("ss_sb", [128, NTT], F32)
        ms_sb = sb("ms_sb", [128, NTT], F32)
        rstd16 = sb("rstd16", [128, NTT], F32)

        ARB = sb("ARB", [128, 56832], BF16)
        ARF = sb("ARF", [128, 4096], F32)

        def vb(c0, n, p=128, p0=0):
            return ARB[p0:p0 + p, c0:c0 + n]

        def vf(c0, n, p=128, p0=0):
            return ARF[p0:p0 + p, c0:c0 + n]

        A_cos, A_sin = 0, 2048
        A_Pe = 4096
        A_OG = 5120
        A_qraw, A_qsw, A_ropw, A_ropz = 7168, 7680, 8192, 8704
        A_wq = [9216, 11264]
        A_wk = [13312, 15360]
        A_wv = [17408, 19456]
        A_wg = [21504, 23552]
        AH_BASE = [25600, 40960]
        AH_QT, AH_KT, AH_Ktok, AH_Vtok, AH_sGT = 0, 2048, 4096, 6144, 8192
        AH_es, AH_est, AH_PT = 10240, 11264, 13312
        P_xbf, P_xsq = 0, 2048
        W_wo, W_ogts, W_wp = 0, 32768, [49152, 50176]
        F_hdn = 0
        F_g = [45056, 47104]
        F_u = [49152, 51200]
        F_d = [45056, 49152]
        F_sg = 53248
        F_sg1 = 53760
        P_xf = [0, 2048]
        R_ang, R_tmp = 0, 2048
        A_cT = 0
        A_enz0 = 2048
        F_pin = [0, 512]
        F_xr = [1024, 1536]
        F_fo = [2048, 2560]
        F_f2 = [3072, 3584]

        psA = psm("psA", [128, 512], F32)
        psB = psm("psB", [128, 512], F32)
        psC = psm("psC", [128, 512], F32)
        psD = psm("psD", [128, 512], F32)
        psE = psm("psE", [128, 2048], F32)
        PS4 = [psA, psB, psC, psD]
        PSK = ["psA", "psB", "psC", "psD"]
        EK = ["psE0", "psE1", "psE2", "psE3"]

        def Ekeys(c0, c1):
            return [f"psE{i}" for i in range(4) if not (c1 <= i * 512 or c0 >= (i + 1) * 512)]

        block = st.enter_context(nc.Block())

        def emit_init():
            tk.dma("qpool", ws1_sb[:, :], ws1t, writes=["ws1"])
            tk.dma("qpool", ws2_sb[:, :], ws2b, writes=["ws2"])
            tk.dma("qsp", gnw_sb[:, :], gnw_in, writes=["gnw"])
            tk.dma("qsp", gs_sb[:, :], gs_in, writes=["gs"])
            tk.dma("qsp", us_sb[:, :], us_in, writes=["us"])
            tk.dma("qsp", ds_sb[:, :], ds_in, writes=["ds"])
            tk.op("pool", lambda g: g.memset(identb[:, :], 1.0), writes=["identb"])
            tk.op("pool", lambda g: g.affine_select(out=identb[:, :], in_=identb[:, :],
                  compare_op=AluOpType.is_equal, fill=0.0, base=0, pattern=[[-1, 128]],
                  channel_multiplier=1), reads=["identb"], writes=["identb"])
            tk.op("pool", lambda g: g.memset(identf64[:, :], 1.0), writes=["identf64"])
            tk.op("pool", lambda g: g.affine_select(out=identf64[:, :], in_=identf64[:, :],
                  compare_op=AluOpType.is_equal, fill=0.0, base=0, pattern=[[-1, 64]],
                  channel_multiplier=1), reads=["identf64"], writes=["identf64"])
            tk.op("pool", lambda g: g.memset(triu_m[:, :], 0.0), writes=["triu"])
            tk.op("pool", lambda g: g.affine_select(out=triu_m[:, :], in_=triu_m[:, :],
                  compare_op=AluOpType.is_gt, fill=1.0, base=0, pattern=[[-1, 128]],
                  channel_multiplier=1), reads=["triu"], writes=["triu"])
            tk.op("pool", lambda g: g.memset(onescol[:, :], 1.0), writes=["onescol"])
            tk.op("pool", lambda g: g.memset(ones_f[:, :], 1.0), writes=["ones_f"])
            tk.op("pool", lambda g: g.memset(negones_f[:, :], -1.0), writes=["negones_f"])
            tk.op("pool", lambda g: g.memset(phalf16[:, :], -0.5), writes=["phalf16"])
            tk.op("pool", lambda g: g.memset(cb32[:, :], 32.0), writes=["cb"])
            tk.op("pool", lambda g: g.memset(cb64[:, :], 64.0), writes=["cb"])
            tk.op("pool", lambda g: g.memset(onesrow_b[:, :], 1.0), writes=["onesrow"])
            tk.dma("qsp", t1aug[16:17, :], onesrow_b[:, :], reads=["onesrow"],
                   writes=["t1aug16"])
            tk.op("pool", lambda g: g.iota(ifcol[:, :], [[1, 1]], base=0,
                  channel_multiplier=1, allow_small_or_imprecise_dtypes=True), writes=["ifcol"])
            tk.op("act", lambda s: s.activation(ifcol[:, :], ifcol[:, :], AF.Exp,
                  scale=-float(np.log(ROPE_BASE)) / 64.0), reads=["ifcol"], writes=["ifcol"])

        def emit_ropetab():
            ang = vf(R_ang, 2048, p=64)
            tmp = vf(R_tmp, 2048, p=64)
            angI = ARF[0:64, R_tmp:R_tmp + 2048].bitcast(I32)
            tk.op("pool", lambda g: g.iota(ang, [[1, T]], base=0, channel_multiplier=0, allow_small_or_imprecise_dtypes=True),
                  writes=["Rang"])
            tk.op("dve", lambda v: v.tensor_scalar(ang, ang, ifcol[:, :], None,
                  AluOpType.mult), reads=["Rang", "ifcol"], writes=["Rang"])
            tk.op("dve", lambda v: v.tensor_scalar(angI, ang, 1.0 / (2 * PI), None,
                  AluOpType.mult), reads=["Rang"], writes=["Rtmp"])
            tk.op("dve", lambda v: v.scalar_tensor_tensor(ang, angI, -2 * PI, ang,
                  AluOpType.mult, AluOpType.add), reads=["Rtmp", "Rang"], writes=["Rang"])
            tk.op("dve", lambda v: v.tensor_scalar(tmp, ang, PI, None, AluOpType.is_gt),
                  reads=["Rang"], writes=["Rtmp"])
            tk.op("dve", lambda v: v.scalar_tensor_tensor(ang, tmp, -2 * PI, ang,
                  AluOpType.mult, AluOpType.add), reads=["Rtmp", "Rang"], writes=["Rang"])
            tk.op("dve", lambda v: v.tensor_scalar(tmp, ang, -PI, None, AluOpType.is_lt),
                  reads=["Rang"], writes=["Rtmp"])
            tk.op("dve", lambda v: v.scalar_tensor_tensor(ang, tmp, 2 * PI, ang,
                  AluOpType.mult, AluOpType.add), reads=["Rtmp", "Rang"], writes=["Rang"])
            tk.op("act", lambda s: s.activation(tmp, ang, AF.Sin), reads=["Rang"],
                  writes=["Rtmp"])
            tk.op("dve", lambda v: v.tensor_copy(ARB[64:128, A_sin:A_sin + 2048], tmp),
                  reads=["Rtmp"], writes=["sinT"])
            tk.op("dve", lambda v: v.tensor_scalar(tmp, tmp, -1.0, None,
                  AluOpType.mult), reads=["Rtmp"], writes=["Rtmp"])
            tk.op("dve", lambda v: v.tensor_copy(vb(A_sin, 2048, p=64), tmp),
                  reads=["Rtmp"], writes=["sinT"])
            tk.op("dve", lambda v: v.tensor_scalar(ang, ang, PI / 2, None, AluOpType.add),
                  reads=["Rang"], writes=["Rang"])
            tk.op("dve", lambda v: v.tensor_scalar(tmp, ang, PI, None, AluOpType.is_gt),
                  reads=["Rang"], writes=["Rtmp"])
            tk.op("dve", lambda v: v.scalar_tensor_tensor(ang, tmp, -2 * PI, ang,
                  AluOpType.mult, AluOpType.add), reads=["Rtmp", "Rang"], writes=["Rang"])
            tk.op("act", lambda s: s.activation(ang, ang, AF.Sin), reads=["Rang"],
                  writes=["Rang"])
            tk.op("dve", lambda v: v.tensor_copy(vb(A_cos, 2048, p=64), ang),
                  reads=["Rang"], writes=["cosT"])
            tk.op("dve", lambda v: v.tensor_copy(ARB[64:128, A_cos:A_cos + 2048], ang),
                  reads=["Rang"], writes=["cosT"])

        def emit_p0(b):
            xbf = vb(P_xbf, 2048)
            xsq = vb(P_xsq, 2048)
            rowA = xnT[0:1, 0:4096].bitcast(F32)
            rowB = xnT[0:1, 4096:8192].bitcast(F32)
            rowC = xnT[0:1, 8192:12288].bitcast(F32)
            for cc in range(NCC):
                bi = cc % 2
                xfv = vf(P_xf[bi], 2048)
                tk.dma("qsp" if bi == 0 else "qact", xfv, xT[b, cc], writes=[f"xf{bi}"])
                tk.op("dve", lambda v, xfv=xfv: v.tensor_copy(xbf, xfv),
                      reads=[f"xf{bi}"], writes=["xbf"])
                tk.op("dve", lambda v: v.tensor_tensor(xsq, xbf, xbf,
                      AluOpType.mult), reads=["xbf"], writes=["xsq"])
                for sl in range(4):
                    c0, c1 = sl * 512, (sl + 1) * 512
                    tk.op("pe", lambda t, c0=c0, c1=c1, cc=cc:
                          t.matmul(psE[0:1, c0:c1], onescol[:, :], xbf[:, c0:c1],
                                   start=(cc == 0), stop=(cc == NCC - 1),
                                   skip_group_check=True),
                          reads=["onescol", "xbf"], writes=Ekeys(c0, c1))
                    tk.op("pe", lambda t, sl=sl, c0=c0, c1=c1, cc=cc:
                          t.matmul(PS4[sl][0:1, :], onescol[:, :], xsq[:, c0:c1],
                                   start=(cc == 0), stop=(cc == NCC - 1),
                                   skip_group_check=True),
                          reads=["onescol", "xsq"], writes=[PSK[sl]])
            tk.op("dve", lambda v: v.tensor_copy(rowA, psE[0:1, :]),
                  reads=EK, writes=["xnT"])
            for sl in range(4):
                tk.op("dve", lambda v, sl=sl: v.tensor_copy(
                      rowB[0:1, sl * 512:(sl + 1) * 512], PS4[sl][0:1, :]),
                      reads=[PSK[sl]], writes=["xnT"])
            tk.op("dve", lambda v: v.tensor_scalar(rowA, rowA, 1.0 / D, None,
                  AluOpType.mult), reads=["xnT"], writes=["xnT"])
            tk.op("dve", lambda v: v.tensor_tensor(rowC, rowA, rowA,
                  AluOpType.mult), reads=["xnT"], writes=["xnT"])
            tk.op("dve", lambda v: v.tensor_scalar(rowB, rowB, 1.0 / D,
                  EPS_LN, AluOpType.mult, AluOpType.add), reads=["xnT"], writes=["xnT"])
            tk.op("dve", lambda v: v.tensor_tensor(rowB, rowB, rowC,
                  AluOpType.subtract), reads=["xnT"], writes=["xnT"])
            tk.op("act", lambda s: s.activation(rowB, rowB, AF.Ln),
                  reads=["xnT"], writes=["xnT"])
            tk.op("act", lambda s: s.activation(rowB, rowB, AF.Exp, scale=-0.5),
                  reads=["xnT"], writes=["xnT"])
            for sl in range(4):
                c0, c1 = sl * 512, (sl + 1) * 512
                tk.op("pe", lambda t, sl=sl, c0=c0, c1=c1: t.matmul(PS4[sl][:, :],
                      ones_f[:, :], rowB[0:1, c0:c1], start=True, stop=True),
                      reads=["ones_f", "xnT"], writes=[PSK[sl]])
            tk.op("dve", lambda v: v.tensor_tensor(rowC, rowA, rowB,
                  AluOpType.mult), reads=["xnT"], writes=["xnT"])
            for sl in range(4):
                c0, c1 = sl * 512, (sl + 1) * 512
                tk.op("pe", lambda t, c0=c0, c1=c1: t.matmul(psE[:, c0:c1], negones_f[:, :],
                      rowC[0:1, c0:c1], start=True, stop=True),
                      reads=["negones_f", "xnT"], writes=Ekeys(c0, c1))
            for cc in range(NCC):
                bi = cc % 2
                xfv = vf(P_xf[bi], 2048)
                tk.dma("qsp" if bi == 0 else "qact", xfv, xT[b, cc], writes=[f"xf{bi}"])
                for sl in range(4):
                    c0, c1 = sl * 512, (sl + 1) * 512
                    tk.op("dve", lambda v, xfv=xfv, c0=c0, c1=c1, sl=sl:
                          v.tensor_tensor(xfv[:, c0:c1], xfv[:, c0:c1], PS4[sl][:, :],
                          AluOpType.mult), reads=[f"xf{bi}", PSK[sl]], writes=[f"xf{bi}"])
                tk.op("dve", lambda v, cc=cc, xfv=xfv: v.tensor_tensor(
                      xnT[:, cc * T:(cc + 1) * T], xfv[:, :], psE[:, :], AluOpType.add),
                      reads=[f"xf{bi}"] + EK, writes=["xnT"])
            for sl in range(4):
                c0, c1 = sl * 512, (sl + 1) * 512
                for cc in range(NCC):
                    tk.op("pe", lambda t, cc=cc, c0=c0, c1=c1:
                          t.matmul(psE[0:16, c0:c1], ws1_sb[:, cc * 16:(cc + 1) * 16],
                                   xnT[:, cc * T + c0:cc * T + c1],
                                   start=(cc == 0), stop=(cc == NCC - 1),
                                   skip_group_check=True),
                          reads=["ws1", "xnT"], writes=Ekeys(c0, c1))
            tk.op("dve", lambda v: v.tensor_copy(t1aug[0:16, :], psE[0:16, :]),
                  reads=EK, writes=["t1aug"])

        def prefetch_head(h):
            p = h % 2
            for (arr, off, key) in ((wq, A_wq[p], f"wq{p}"), (wk, A_wk[p], f"wk{p}"),
                                    (wv, A_wv[p], f"wv{p}"), (wg, A_wg[p], f"wg{p}")):
                tk.dma("qpool", vb(off, 2048), arr[h], writes=[key])

        def hv(h):
            p = h % 2
            base = AH_BASE[p]
            return dict(
                p=p,
                QT=vb(base + AH_QT, 2048), KT=vb(base + AH_KT, 2048),
                Ktok=vb(base + AH_Ktok, 2048), Vtok=vb(base + AH_Vtok, 2048),
                sGT=vb(base + AH_sGT, 2048), esv=vb(base + AH_es, 1024),
                est=vb(base + AH_est, 2048, p=64), PT=vb(base + AH_PT, 2048, p=64),
                kQT=f"QT{p}", kKT=f"KT{p}", kKtok=f"Ktok{p}", kVtok=f"Vtok{p}",
                ksGT=f"sGT{p}", kes=f"es{p}", kest=f"est{p}", kPT=f"PT{p}",
                enz=vf(A_enz0 + p * 1024, 1024), kenz=f"enz{p}",
                tok=vf(A_enz0 + (1 - p) * 1024, 1024), ktok=f"enz{1 - p}",
            )

        def s1_gen(b, h):
            v = hv(h)
            p = v["p"]
            if h + 1 < H:
                prefetch_head(h + 1)
                yield
            wqv = vb(A_wq[p], 2048)
            wkv = vb(A_wk[p], 2048)
            wvv = vb(A_wv[p], 2048)
            wgv = vb(A_wg[p], 2048)
            qraw = vb(A_qraw, 512)
            qsw = vb(A_qsw, 512)
            ropw = vb(A_ropw, 512)
            ropz = vb(A_ropz, 512)
            PSF = [psA, psB]
            KSF = ["psA", "psB"]
            esv = v["esv"]
            for w in range(2):
                ps = PSF[w]
                kk = KSF[w]
                po = PSF[1 - w]
                ko = KSF[1 - w]
                for t8 in range(8):
                    tt = 8 * w + t8
                    tk.op("pe", lambda t, ps=ps, t8=t8, tt=tt: t.matmul(
                          ps[:, t8 * 64:(t8 + 1) * 64],
                          t1aug[:, tt * 128:(tt + 1) * 128],
                          ws2_sb[:, h * M:(h + 1) * M],
                          start=(t8 == 0), stop=(t8 == 7), skip_group_check=True),
                          reads=["t1aug", "t1aug16", "ws2"], writes=[kk])
                    if t8 % 4 == 3:
                        yield
                tk.op("act", lambda s, ps=ps, po=po: s.activation(po[:, :], ps[:, :],
                      AF.Relu, bias=cb32[:, :]), reads=[kk, "cb"], writes=[ko])
                yield
                tk.op("act", lambda s, ps=ps, po=po: s.activation(ps[:, :], po[:, :],
                      AF.Relu, scale=-1.0, bias=cb64[:, :]), reads=[ko, "cb"],
                      writes=[kk])
                yield
                tk.op("act", lambda s, ps=ps, w=w: s.activation(
                      esv[:, w * 512:(w + 1) * 512], ps[:, :], AF.Exp,
                      scale=-1.0, bias=cb32[:, :]), reads=[kk, "cb"], writes=[v["kes"]])
                yield
            for w in range(2):
                ps = PSF[w]
                kk = KSF[w]
                for t8 in range(8):
                    tt = 8 * w + t8
                    ov = ps[0:64, t8 * 64:(t8 + 1) * 64].bitcast(BF16)
                    tk.op("pe", lambda t, ov=ov, tt=tt, esv=esv: t.transpose(ov,
                          esv[:, tt * 64:(tt + 1) * 64], identb[:, :]),
                          reads=[v["kes"], "identb"], writes=[kk])
                    if t8 % 4 == 3:
                        yield
                tk.op("dve", lambda v_, ps=ps, w=w: v_.tensor_copy(
                      v["est"][:, w * 1024:(w + 1) * 1024], ps[0:64, :].bitcast(BF16)),
                      reads=[kk], writes=[v["kest"]])
                yield
            cTv = vf(A_cT, 2048, p=64)
            for q in range(4):
                c0 = q * 512
                init = 0.0 if q == 0 else cTv[:, c0 - 1:c0]
                tk.op("dve", lambda v_, v=v, c0=c0, init=init: v_.tensor_tensor_scan(
                      cTv[:, c0:c0 + 512], v["est"][:, c0:c0 + 512],
                      v["est"][:, c0:c0 + 512], init, AluOpType.add, AluOpType.bypass),
                      reads=[v["kest"], "cT"], writes=["cT"])
                yield
            for (wv_, out, wkey, okey) in ((wqv, v["QT"], f"wq{p}", v["kQT"]),
                                           (wkv, v["KT"], f"wk{p}", v["kKT"])):
                for sl in range(4):
                    c0 = sl * 512
                    ps = PSF[sl % 2]
                    kk = KSF[sl % 2]
                    for cc in range(NCC):
                        tk.op("pe", lambda t, ps=ps, wv_=wv_, cc=cc, c0=c0:
                              t.matmul(ps[:, :], wv_[:, cc * DK:cc * DK + 128],
                                       xnT[:, cc * T + c0:cc * T + c0 + 512],
                                       start=(cc == 0), stop=(cc == NCC - 1)),
                              reads=[wkey, "xnT"], writes=[kk])
                        if cc % 4 == 3:
                            yield
                    tk.op("act", lambda s, ps=ps: s.activation(qraw[:, :], ps[:, :],
                          AF.Copy), reads=[kk], writes=["qraw"])
                    yield
                    tk.op("dve", lambda v_: v_.tensor_copy(qsw[0:64, :],
                          qraw[64:128, :]), reads=["qraw"], writes=["qsw"])
                    tk.op("dve", lambda v_: v_.tensor_copy(qsw[64:128, :],
                          qraw[0:64, :]), reads=["qraw"], writes=["qsw"])
                    yield
                    tk.op("dve", lambda v_, c0=c0: v_.tensor_tensor(ropw[:, :],
                          qraw[:, :], vb(A_cos + c0, 512), AluOpType.mult),
                          reads=["qraw", "cosT"], writes=["ropw"])
                    tk.op("pool", lambda g, c0=c0: g.tensor_tensor(ropz[:, :], qsw[:, :],
                          vb(A_sin + c0, 512), AluOpType.mult),
                          reads=["qsw", "sinT"], writes=["ropz"])
                    yield
                    tk.op("dve", lambda v_, out=out, c0=c0: v_.tensor_tensor(
                          out[:, c0:c0 + 512], ropw[:, :], ropz[:, :], AluOpType.add),
                          reads=["ropw", "ropz"], writes=[okey])
                    yield
            ktokreg = v["Ktok"]
            for sl in range(4):
                c0 = sl * 512
                ps = PSF[sl % 2]
                kk = KSF[sl % 2]
                for cc in range(NCC):
                    tk.op("pe", lambda t, ps=ps, cc=cc, c0=c0, wvv=wvv:
                          t.matmul(ps[:, :], wvv[:, cc * DK:cc * DK + 128],
                                   xnT[:, cc * T + c0:cc * T + c0 + 512],
                                   start=(cc == 0), stop=(cc == NCC - 1)),
                          reads=[f"wv{p}", "xnT"], writes=[kk])
                    if cc % 4 == 3:
                        yield
                tk.op("act", lambda s, ps=ps, c0=c0: s.activation(
                      ktokreg[:, c0:c0 + 512], ps[:, :], AF.Copy),
                      reads=[kk], writes=[v["kKtok"]])
                yield
            for tt in range(NTT):
                ps = PSF[tt % 2]
                kk = KSF[tt % 2]
                pview = ps[:, 0:64].bitcast(BF16)
                tk.op("pe", lambda t, pview=pview, tt=tt: t.transpose(pview,
                      ktokreg[:, tt * 128:(tt + 1) * 128], identb[:, :]),
                      reads=[v["kKtok"], "identb"], writes=[kk])
                tk.op("act", lambda s, pview=pview, tt=tt: s.activation(
                      v["Vtok"][:, tt * 128:(tt + 1) * 128], pview, AF.Copy),
                      reads=[kk], writes=[v["kVtok"]])
                if tt % 2 == 1:
                    yield
            for sl in range(4):
                c0 = sl * 512
                ps = PSF[sl % 2]
                kk = KSF[sl % 2]
                for cc in range(NCC):
                    tk.op("pe", lambda t, ps=ps, cc=cc, c0=c0, wgv=wgv:
                          t.matmul(ps[:, :], wgv[:, cc * DK:cc * DK + 128],
                                   xnT[:, cc * T + c0:cc * T + c0 + 512],
                                   start=(cc == 0), stop=(cc == NCC - 1)),
                          reads=[f"wg{p}", "xnT"], writes=[kk])
                    if cc % 4 == 3:
                        yield
                tk.op("act", lambda s, ps=ps: s.activation(ropw[:, :], ps[:, :], AF.Tanh,
                      scale=0.5), reads=[kk], writes=["ropw"])
                yield
                tk.op("dve", lambda v_: v_.tensor_scalar(ropz[:, :], ropw[:, :], 0.5, 0.5,
                      AluOpType.mult, AluOpType.add), reads=["ropw"], writes=["ropz"])
                yield
                tk.op("dve", lambda v_, ps=ps, c0=c0, kk=kk: v_.tensor_tensor(
                      v["sGT"][:, c0:c0 + 512], ropz[:, :], ps[:, :], AluOpType.mult),
                      reads=["ropz", kk], writes=[v["ksGT"]])
                yield
            for tt in range(NTT):
                ps = PSF[tt % 2]
                kk = KSF[tt % 2]
                pview = ps[:, 0:64].bitcast(BF16)
                tk.op("pe", lambda t, pview=pview, tt=tt: t.transpose(pview,
                      v["KT"][:, tt * 128:(tt + 1) * 128], identb[:, :]),
                      reads=[v["kKT"], "identb"], writes=[kk])
                tk.op("act", lambda s, pview=pview, tt=tt: s.activation(
                      ktokreg[:, tt * 128:(tt + 1) * 128], pview, AF.Copy),
                      reads=[kk], writes=[v["kKtok"]])
                if tt % 2 == 1:
                    yield
            yield "enzbar"
            for w in range(2):
                ps = PSF[w]
                kk = KSF[w]
                for t8 in range(8):
                    tt = 8 * w + t8
                    tk.op("pe", lambda t, ps=ps, t8=t8, tt=tt: t.transpose(
                          ps[:, t8 * 64:(t8 + 1) * 64],
                          cTv[:, tt * 128:(tt + 1) * 128], identf64[:, :]),
                          reads=["cT", "identf64"], writes=[kk])
                    if t8 % 4 == 3:
                        yield
                lntmp = vf(A_cT, 512)
                tk.op("act", lambda s, ps=ps, lntmp=lntmp: s.activation(lntmp, ps[:, :],
                      AF.Ln), reads=[kk], writes=["cT"])
                yield
                tk.op("act", lambda s, w=w, v=v, lntmp=lntmp: s.activation(
                      v["enz"][:, w * 512:(w + 1) * 512], lntmp, AF.Exp, scale=-1.0),
                      reads=["cT"], writes=[v["kenz"]])
                yield

        def s2a(b, h, pump):
            v = hv(h)
            esv = v["esv"]
            for w in range(4):
                atm4 = vb(A_Pe + (w % 2) * 512, 512)
                for k in range(4):
                    i = 4 * w + k
                    tk.op("pe", lambda t, k=k, i=i, v=v: t.matmul(
                          psD[:, k * 128:(k + 1) * 128],
                          v["KT"][:, i * 128:(i + 1) * 128],
                          v["QT"][:, i * 128:(i + 1) * 128],
                          start=(k == 0), stop=(k == 3), skip_group_check=True),
                          reads=[v["kKT"], v["kQT"]], writes=["psD"])
                pump(2)
                d3 = psD[:, :].rearrange("p (k f) -> p k f", f=128)
                a3 = atm4.rearrange("p (k f) -> p k f", f=128)
                tk.op("dve", lambda v_, a3=a3, d3=d3: v_.scalar_tensor_tensor(
                      a3, d3, SCALE,
                      triu_m[:, :].unsqueeze(1).broadcast_to([128, 4, 128]),
                      AluOpType.mult, AluOpType.mult),
                      reads=["psD", "triu"], writes=["Pe"])
                pump(2)
                for k in range(4):
                    i = 4 * w + k
                    okv = psE[:, i * 64:(i + 1) * 64]
                    okk = Ekeys(i * 64, (i + 1) * 64)
                    if i > 0:
                        tk.op("pe", lambda t, okv=okv, i=i, v=v: t.matmul(okv,
                              v["QT"][:, i * 128:(i + 1) * 128],
                              cke_sb[(i - 1) % 2][:, :],
                              start=(i == 8), stop=False, skip_group_check=True),
                              reads=[v["kQT"], f"cke{(i - 1) % 2}"], writes=okk)
                    tk.op("pe", lambda t, okv=okv, i=i, k=k, atm4=atm4, esv=esv:
                          t.matmul(okv, atm4[:, k * 128:(k + 1) * 128],
                                   esv[:, i * 64:(i + 1) * 64],
                                   start=(i == 0), stop=True, skip_group_check=True),
                          reads=["Pe", v["kes"]], writes=okk)
                    if i + 1 < NTT:
                        tk.op("pe", lambda t, i=i, v=v, esv=esv: t.matmul(psC[:, 0:64],
                              v["Ktok"][:, i * 128:(i + 1) * 128],
                              esv[:, i * 64:(i + 1) * 64],
                              start=(i == 0), stop=(i == NTT - 2), skip_group_check=True),
                              reads=[v["kKtok"], v["kes"]], writes=["psC"])
                        tk.op("dve", lambda v_, i=i: v_.tensor_scalar(
                              cke_sb[i % 2][:, :], psC[:, 0:64], SCALE, None,
                              AluOpType.mult), reads=["psC"], writes=[f"cke{i % 2}"])
                    pump(2)

        def s2b(b, h, pump):
            v = hv(h)
            enzv = v["enz"]
            tokv = v["tok"]
            ktok = v["ktok"]
            Pev = vb(A_Pe, 1024)
            tok3 = tokv.rearrange("p (i m) -> p i m", m=M)
            e3 = Pev.rearrange("p (i m) -> p i m", m=M)
            tk.op("dve", lambda v_: v_.tensor_tensor(tokv, psE[:, 0:1024], enzv,
                  AluOpType.mult), reads=["psE0", "psE1", v["kenz"]], writes=[ktok])
            pump(3)
            tk.op("dve", lambda v_: v_.tensor_reduce(negmax_sb[:, :], tok3, axis=AX.X,
                  op=AluOpType.max, negate=True), reads=[ktok], writes=["negmax"])
            pump(3)
            tk.op("dve", lambda v_: v_.tensor_tensor(tok3, tok3,
                  negmax_sb[:, :].unsqueeze(2).broadcast_to([128, NTT, M]),
                  AluOpType.add), reads=[ktok, "negmax"], writes=[ktok])
            pump(3)
            tk.op("act", lambda s: s.activation(Pev, tokv, AF.Exp), reads=[ktok],
                  writes=["Pe"])
            pump(3)
            tk.op("dve", lambda v_: v_.tensor_reduce(rowsum_sb[:, :], e3, axis=AX.X,
                  op=AluOpType.add), reads=["Pe"], writes=["rowsum"])
            pump(3)
            tk.op("dve", lambda v_: v_.reciprocal(rcp16_sb[:, :], rowsum_sb[:, :]),
                  reads=["rowsum"], writes=["rcp16"])
            pump(3)
            tk.op("dve", lambda v_: v_.tensor_tensor(tok3, e3,
                  rcp16_sb[:, :].unsqueeze(2).broadcast_to([128, NTT, M]),
                  AluOpType.mult), reads=["Pe", "rcp16"], writes=[ktok])
            pump(3)
            tk.op("dve", lambda v_: v_.tensor_tensor(Pev, tokv, enzv, AluOpType.mult),
                  reads=[ktok, v["kenz"]], writes=["Pe"])
            pump(3)
            for tt in range(NTT):
                ov = psE[0:64, tt * 64:(tt + 1) * 64].bitcast(BF16)
                tk.op("pe", lambda t, ov=ov, tt=tt, Pev=Pev: t.transpose(ov,
                      Pev[:, tt * 64:(tt + 1) * 64], identb[:, :]),
                      reads=["Pe", "identb"], writes=Ekeys(tt * 64, (tt + 1) * 64))
                if tt % 4 == 3:
                    pump(2)
            tk.op("dve", lambda v_, v=v: v_.tensor_copy(v["PT"],
                  psE[0:64, 0:1024].bitcast(BF16)), reads=["psE0", "psE1"],
                  writes=[v["kPT"]])
            pump(2)

        def s3(b, h, pump):
            v = hv(h)
            esv = v["esv"]
            for w in range(4):
                w2m4 = vb(A_Pe + (w % 2) * 512, 512)
                for k in range(4):
                    i = 4 * w + k
                    tk.op("pe", lambda t, k=k, i=i, v=v: t.matmul(
                          psD[:, k * 128:(k + 1) * 128],
                          v["est"][:, i * 128:(i + 1) * 128],
                          v["PT"][:, i * 128:(i + 1) * 128],
                          start=(k == 0), stop=(k == 3), skip_group_check=True),
                          reads=[v["kest"], v["kPT"]], writes=["psD"])
                pump(2)
                d3 = psD[:, :].rearrange("p (k f) -> p k f", f=128)
                a3 = w2m4.rearrange("p (k f) -> p k f", f=128)
                tk.op("dve", lambda v_, a3=a3, d3=d3: v_.tensor_tensor(
                      a3, d3, triu_m[:, :].unsqueeze(1).broadcast_to([128, 4, 128]),
                      AluOpType.mult), reads=["psD", "triu"], writes=["Pe"])
                pump(2)
                for k in range(4):
                    i = 4 * w + k
                    ov = psE[:, i * 128:(i + 1) * 128]
                    okk = Ekeys(i * 128, (i + 1) * 128)
                    if i > 0:
                        tk.op("pe", lambda t, ov=ov, i=i, v=v: t.matmul(ov,
                              v["PT"][:, i * 128:(i + 1) * 128],
                              cev_sb[(i - 1) % 2][:, :],
                              start=(i % 4 == 0), stop=False, skip_group_check=True),
                              reads=[v["kPT"], f"cev{(i - 1) % 2}"], writes=okk)
                    tk.op("pe", lambda t, ov=ov, i=i, k=k, w2m4=w2m4, v=v:
                          t.matmul(ov, w2m4[:, k * 128:(k + 1) * 128],
                                   v["Vtok"][:, i * 128:(i + 1) * 128],
                                   start=(i == 0), stop=True, skip_group_check=True),
                          reads=["Pe", v["kVtok"]], writes=okk)
                    if i + 1 < NTT:
                        tk.op("pe", lambda t, i=i, v=v, esv=esv: t.matmul(
                              psC[0:64, 0:128], esv[:, i * 64:(i + 1) * 64],
                              v["Vtok"][:, i * 128:(i + 1) * 128],
                              start=(i == 0), stop=(i == NTT - 2), skip_group_check=True),
                              reads=[v["kes"], v["kVtok"]], writes=["psC"])
                        tk.op("dve", lambda v_, i=i: v_.tensor_copy(cev_sb[i % 2][:, :],
                              psC[0:64, 0:128]), reads=["psC"], writes=[f"cev{i % 2}"])
                    pump(2)

        def s4(b, h, pump, dump=None):
            v = hv(h)
            OG = vb(A_OG, 2048)
            Pev = vb(A_Pe, 1024)
            tk.op("act", lambda s: s.activation(OG[:, :], psE[:, :], AF.Copy),
                  reads=EK, writes=["OG"])
            pump(2)
            for w in range(2):
                tk.op("dve", lambda v_, w=w: v_.tensor_tensor(Pev,
                      OG[:, w * 1024:(w + 1) * 1024], OG[:, w * 1024:(w + 1) * 1024],
                      AluOpType.mult), reads=["OG"], writes=["Pe"])
                pump(2)
                tk.op("dve", lambda v_, w=w: v_.tensor_reduce(ss_sb[:, w * 8:(w + 1) * 8],
                      Pev.rearrange("p (i f) -> p i f", f=128), axis=AX.X,
                      op=AluOpType.add), reads=["Pe"], writes=["ss"])
                pump(2)
            tk.op("dve", lambda v_: v_.tensor_scalar(ms_sb[:, :], ss_sb[:, :], 1.0 / 128,
                  EPS_RMS, AluOpType.mult, AluOpType.add), reads=["ss"], writes=["ms"])
            tk.op("pool", lambda g: g.tensor_tensor(rstd16[:, :], ms_sb[:, :],
                  phalf16[:, :], AluOpType.pow), reads=["ms", "phalf16"],
                  writes=["rstd16"])
            pump(2)
            for w in range(2):
                for k in range(8):
                    i = 8 * w + k
                    tk.op("dve", lambda v_, i=i, k=k: v_.tensor_scalar(
                          on_sb[i % 4][:, :], psE[:, i * 128:(i + 1) * 128],
                          rstd16[:, i:i + 1], None, AluOpType.mult),
                          reads=Ekeys(i * 128, (i + 1) * 128) + ["rstd16"],
                          writes=[f"on{i % 4}"])
                    ov = psD[:, k * 64:(k + 1) * 64].bitcast(BF16)
                    tk.op("pe", lambda t, ov=ov, i=i: t.transpose(ov,
                          on_sb[i % 4][:, :], identb[:, :]),
                          reads=[f"on{i % 4}", "identb"], writes=["psD"])
                    pump(1)
                tk.op("dve", lambda v_, w=w, v=v: v_.scalar_tensor_tensor(
                      OG[:, w * 1024:(w + 1) * 1024], psD[:, :].bitcast(BF16),
                      gnw_sb[:, :], v["sGT"][:, w * 1024:(w + 1) * 1024],
                      AluOpType.mult, AluOpType.mult),
                      reads=["psD", "gnw", v["ksGT"], "OG"], writes=["OG"])
                pump(2)
            tk.dma("qsp", ogt_s[:, h, :], OG[:, :], reads=["OG"], writes=[f"ogt{h}"])
            if dump:
                for name, ap, keys in dump:
                    if ap is not None:
                        tk.dma("qsp", dt[name], ap, reads=keys)

        def emit_w(b):
            wov = vb(W_wo, 32768)
            OGB = [W_ogts, W_ogts + 8192]
            for hc in range(H):
                dst = vb(W_wo + hc * 2048, 2048)
                tk.dma("qpool", dst, wo[hc], writes=[f"wo{hc}"])
            dst0 = vb(OGB[0], 8192).rearrange("p (h c) -> p h c", c=512)
            tk.dma("qsp", dst0, ogt_s[:, :, 0:512],
                   reads=[f"ogt{hh}" for hh in range(H)], writes=["ogts0"])
            for ts in range(4):
                ogts = vb(OGB[ts % 2], 8192)
                if ts + 1 < 4:
                    dstn = vb(OGB[(ts + 1) % 2], 8192).rearrange("p (h c) -> p h c", c=512)
                    tk.dma("qsp", dstn, ogt_s[:, :, (ts + 1) * 512:(ts + 2) * 512],
                           reads=[f"ogt{hh}" for hh in range(H)],
                           writes=[f"ogts{(ts + 1) % 2}"])
                for dtt in range(NCC):
                    ps = PS4[dtt % 4]
                    for hc in range(H):
                        tk.op("pe", lambda t, ps=ps, hc=hc, dtt=dtt, ogts=ogts:
                              t.matmul(ps[:, :],
                              wov[:, hc * 2048 + dtt * 128:hc * 2048 + (dtt + 1) * 128],
                              ogts[:, hc * 512:(hc + 1) * 512],
                              start=(hc == 0), stop=(hc == H - 1)),
                              reads=[f"wo{hc}", f"ogts{ts % 2}"], writes=[PSK[dtt % 4]])
                    wp = ARB[:, W_wp[dtt % 2]:W_wp[dtt % 2] + 1024].bitcast(F32)
                    tk.op("dve", lambda v, ps=ps, wp=wp, dtt=dtt: v.tensor_copy(wp, ps[:, :]),
                          reads=[PSK[dtt % 4]], writes=[f"wpart{dtt % 2}"])
                    tk.dma("qsp", part_s[dtt, :, ts * 512:(ts + 1) * 512], wp,
                           reads=[f"wpart{dtt % 2}"], writes=[f"part{dtt}_{ts}"])

        def emit_f(b, r):
            ic0 = r * ICB
            sgv = vb(F_sg, 512)
            sg2 = [vb(F_sg, 512), vb(F_sg1, 512)]
            tk.dma("qpool", vb(F_g[0], 2048), gate[ic0], writes=["gch0"])
            tk.dma("qpool", vb(F_u[0], 2048), upw[ic0], writes=["uch0"])
            for k in range(ICB):
                ic = ic0 + k
                p = k % 2
                gv = vb(F_g[p], 2048)
                uv = vb(F_u[p], 2048)
                if k + 1 < ICB:
                    tk.dma("qpool", vb(F_g[1 - p], 2048), gate[ic + 1],
                           writes=[f"gch{1 - p}"])
                    tk.dma("qpool", vb(F_u[1 - p], 2048), upw[ic + 1],
                           writes=[f"uch{1 - p}"])
                for sl in range(4):
                    c0 = sl * 512
                    psg = [psA, psC][sl % 2]
                    psu = [psB, psD][sl % 2]
                    kg = ["psA", "psC"][sl % 2]
                    ku = ["psB", "psD"][sl % 2]
                    for cc in range(NCC):
                        tk.op("pe", lambda t, psg=psg, gv=gv, cc=cc, c0=c0: t.matmul(
                              psg[:, :], gv[:, cc * 128:(cc + 1) * 128],
                              xnT[:, cc * T + c0:cc * T + c0 + 512],
                              start=(cc == 0), stop=(cc == NCC - 1)),
                              reads=[f"gch{p}", "xnT"], writes=[kg])
                        tk.op("pe", lambda t, psu=psu, uv=uv, cc=cc, c0=c0: t.matmul(
                              psu[:, :], uv[:, cc * 128:(cc + 1) * 128],
                              xnT[:, cc * T + c0:cc * T + c0 + 512],
                              start=(cc == 0), stop=(cc == NCC - 1)),
                              reads=[f"uch{p}", "xnT"], writes=[ku])
                    sgs = [sgv, vb(F_sg, 512)][0]
                    tk.op("act", lambda s, psg=psg, ic=ic, sl=sl: s.activation(
                          sg2[sl % 2], psg[:, :], AF.Silu, scale=gs_sb[:, ic:ic + 1]),
                          reads=[kg, "gs"], writes=[f"sg{sl % 2}"])
                    tk.op("dve", lambda v, psu=psu, k=k, c0=c0, ic=ic, sl=sl:
                          v.scalar_tensor_tensor(
                          vb(F_hdn + k * 2048 + c0, 512), psu[:, :], us_sb[:, ic:ic + 1],
                          sg2[sl % 2], AluOpType.mult, AluOpType.mult),
                          reads=[ku, "us", f"sg{sl % 2}"], writes=["hdn"])

            psE_sl = [psE[:, e * 512:(e + 1) * 512] for e in range(4)]
            tk.dma("qpool", vb(F_d[0], ICB * 128),
                   down[0, :, ic0 * 128:(ic0 + ICB) * 128], writes=["gch0", "gch1"])
            for ht in range(NCC):
                p = ht % 2
                dv = vb(F_d[p], ICB * 128)
                if ht + 1 < NCC:
                    tk.dma("qpool", vb(F_d[1 - p], ICB * 128),
                           down[ht + 1, :, ic0 * 128:(ic0 + ICB) * 128],
                           writes=(["uch0", "uch1"] if (ht + 1) % 2 else ["gch0", "gch1"]))
                for sl in range(4):
                    g = ht * 4 + sl
                    if g % 6 < 2:
                        ps, pskeys = [psC, psD][g % 6][:, :], [PSK[2 + g % 6]]
                    else:
                        ps, pskeys = psE_sl[g % 6 - 2], [EK[g % 6 - 2]]
                    c0 = sl * 512
                    pin = vf(F_pin[sl % 2], 512)
                    fo = vf(F_fo[sl % 2], 512)
                    tk.dma("qsp", pin, part_s[ht, :, c0:c0 + 512],
                           reads=[f"part{ht}_{sl}"], writes=[f"pin{sl % 2}"])
                    if r == 1:
                        xr = vf(F_xr[sl % 2], 512)
                        tk.dma("qsp", xr, xT[b, ht, :, c0:c0 + 512], writes=[f"xr{sl % 2}"])
                    for k in range(ICB):
                        tk.op("pe", lambda t, ps=ps, dv=dv, k=k, c0=c0: t.matmul(ps,
                              dv[:, k * 128:(k + 1) * 128],
                              vb(F_hdn + k * 2048 + c0, 512),
                              start=(k == 0), stop=(k == ICB - 1)),
                              reads=(["uch0", "uch1"] if p else ["gch0", "gch1"]) + ["hdn"],
                              writes=pskeys)
                    tk.op("dve", lambda v, ps=ps, pin=pin, fo=fo, ht=ht, pskeys=pskeys:
                          v.scalar_tensor_tensor(fo, ps, ds_sb[:, ht:ht + 1], pin,
                          AluOpType.mult, AluOpType.add),
                          reads=pskeys + ["ds", f"pin{sl % 2}"], writes=[f"fo{sl % 2}"])
                    if r == 0:
                        tk.dma("qsp", part_s[ht, :, c0:c0 + 512], fo,
                               reads=[f"fo{sl % 2}"], writes=[f"part{ht}_{sl}"])
                    else:
                        xr = vf(F_xr[sl % 2], 512)
                        f2 = vf(F_f2[sl % 2], 512)
                        tk.op("dve", lambda v, fo=fo, xr=xr, f2=f2: v.tensor_tensor(
                              f2, fo, xr, AluOpType.add),
                              reads=[f"fo{sl % 2}", f"xr{sl % 2}"], writes=[f"f2{sl % 2}"])
                        tk.dma("qsp", outT[b, ht, :, c0:c0 + 512], f2,
                               reads=[f"f2{sl % 2}"])

        emit_init()
        for b in range(nbat):
            if "p0" in phases:
                emit_p0(b)
            tk.fence()
            if "attn" in phases:
                emit_ropetab()
                tk.fence()
                prefetch_head(0)
                for _ in s1_gen(b, 0):
                    pass
                fill = {"g": None, "blocked": False}

                def pump(n):
                    if fill["blocked"]:
                        return
                    g = fill["g"]
                    if g is None:
                        return
                    for _ in range(n):
                        try:
                            val = next(g)
                        except StopIteration:
                            fill["g"] = None
                            return
                        if val == "enzbar":
                            fill["blocked"] = True
                            return

                for h in range(nheads):
                    fill["g"] = s1_gen(b, h + 1) if h + 1 < nheads else None
                    dump = None
                    if b == 0 and h == 0 and dbg and "h0" in dbg:
                        v0 = hv(0)
                        dump = []
                        for nm, ap, key in (("dbg_QT", v0["QT"], v0["kQT"]),
                                            ("dbg_KT", v0["KT"], v0["kKT"]),
                                            ("dbg_Vtok", v0["Vtok"], v0["kVtok"]),
                                            ("dbg_sGT", v0["sGT"], v0["ksGT"]),
                                            ("dbg_es", v0["esv"], v0["kes"]),
                                            ("dbg_OG", vb(A_OG, 2048), "OG")):
                            dout(nm, [ap.shape[0], ap.shape[1]], BF16)
                            dump.append((nm, ap, [key]))
                        dout("dbg_enz", [128, 1024], F32)
                        dump.append(("dbg_enz", v0["enz"], [v0["kenz"]]))
                        dout("dbg_ss", [128, NTT], F32)
                        dump.append(("dbg_ss", ss_sb[:, :], ["ss"]))
                        dout("dbg_rstd", [128, NTT], F32)
                        dump.append(("dbg_rstd", rstd16[:, :], ["rstd16"]))
                    s2a(b, h, pump)
                    s2b(b, h, pump)
                    fill["blocked"] = False
                    s3(b, h, pump)
                    s4(b, h, pump, dump=dump)
                    fill["blocked"] = False
                    pump(10 ** 9)
                tk.fence()
            if "w" in phases:
                emit_w(b)
                tk.fence()
            if "ffn" in phases:
                emit_f(b, 0)
                emit_f(b, 1)
                tk.fence()
            if b == 0 and dbg:
                if "xnT" in dbg:
                    o = dout("dbg_xnT", [128, NCC * T], BF16)
                    tk.dma("qsp", o, xnT[:, :], reads=["xnT"])
                if "t1aug" in dbg:
                    o = dout("dbg_t1", [17, T], BF16)
                    tk.dma("qsp", o, t1aug[:, :], reads=["t1aug", "t1aug16"])
                if "ropetab" in dbg:
                    o1 = dout("dbg_cos", [128, T], BF16)
                    o2 = dout("dbg_sin", [128, T], BF16)
                    tk.dma("qsp", o1, vb(A_cos, 2048), reads=["cosT"])
                    tk.dma("qsp", o2, vb(A_sin, 2048), reads=["sinT"])
                if "part" in dbg:
                    o = dout("dbg_part", [NCC, 128, T], F32)
                    tk.dma("qsp", o, part_s,
                           reads=[f"part{ht}_{sl}" for ht in range(NCC) for sl in range(4)])
        tk.fence_all("sp")
        tk.finalize_incs()

        @block.gpsimd
        def _(g):
            tk.replay("pool", g)

        @block.tensor
        def _(t):
            tk.replay("pe", t)

        @block.vector
        def _(v):
            tk.replay("dve", v)

        @block.scalar
        def _(s):
            tk.replay("act", s)

        @block.sync
        def _(sp):
            tk.replay("sp", sp)

    return nc


def host_layout(inputs):
    x = np.ascontiguousarray(inputs["L0_attn_hidden_states"])
    Wq = inputs["L0_attn_Wq"]; Wk = inputs["L0_attn_Wk"]
    Wv = inputs["L0_attn_Wv"]; Wg = inputs["L0_attn_Wg"]
    Wo = inputs["L0_attn_Wo"]; Ws1 = inputs["L0_attn_Ws1"]
    Ws2 = inputs["L0_attn_Ws2"]; bs2 = inputs["L0_attn_bs2"]
    gnw = inputs["L0_attn_g_norm_weight"]
    gw = inputs["L0_ffn_gate_w"]; uw = inputs["L0_ffn_up_w"]; dw = inputs["L0_ffn_down_w"]
    gsc = inputs["L0_ffn_gate_s"]; usc = inputs["L0_ffn_up_s"]; dsc = inputs["L0_ffn_down_s"]

    def wT_heads(W):
        t = np.ascontiguousarray(W).T.reshape(NCC, 128, H, DK)
        return np.ascontiguousarray(t.transpose(2, 1, 0, 3).reshape(H, 128, NCC * DK))

    com = {
        "wq": wT_heads(Wq), "wk": wT_heads(Wk), "wv": wT_heads(Wv), "wg": wT_heads(Wg),
        "wo": np.ascontiguousarray(np.ascontiguousarray(Wo).T.reshape(H, 128, D)),
        "ws1t": np.ascontiguousarray(np.ascontiguousarray(Ws1).T.reshape(
            NCC, 128, 16).transpose(1, 0, 2).reshape(128, NCC * 16)),
        "ws2b": np.ascontiguousarray(np.concatenate(
            [np.ascontiguousarray(Ws2).T, np.asarray(bs2)[None, :]], axis=0)),
        "gnw": np.ascontiguousarray(np.asarray(gnw).reshape(128, 1)),
        "gate": np.ascontiguousarray(np.ascontiguousarray(gw).T.reshape(
            NCC, 128, NIC, 128).transpose(2, 1, 0, 3).reshape(NIC, 128, NCC * 128)),
        "upw": np.ascontiguousarray(np.ascontiguousarray(uw).T.reshape(
            NCC, 128, NIC, 128).transpose(2, 1, 0, 3).reshape(NIC, 128, NCC * 128)),
        "down": np.ascontiguousarray(np.ascontiguousarray(dw).T.reshape(
            NIC, 128, NCC, 128).transpose(2, 1, 0, 3).reshape(NCC, 128, NIC * 128)),
        "gs": np.ascontiguousarray(np.asarray(gsc).reshape(NIC, 128).T),
        "us": np.ascontiguousarray(np.asarray(usc).reshape(NIC, 128).T),
        "ds": np.ascontiguousarray(np.asarray(dsc).reshape(NCC, 128).T),
    }
    in_maps = []
    for core in range(8):
        m = dict(com)
        xb = x[2 * core:2 * core + 2]
        m["xT"] = np.ascontiguousarray(xb.transpose(0, 2, 1).reshape(NB, NCC, 128, T))
        in_maps.append(m)
    return in_maps


def kernel(**inputs):
    in_maps = host_layout(inputs)
    nc = build_program()
    res = run_bass_kernel_spmd(nc, in_maps, core_ids=list(range(8)))
    out = np.empty((16, T, D), dtype=np.float32)
    for core in range(8):
        oT = np.asarray(res.results[core]["outT"])
        for b in range(NB):
            out[2 * core + b] = oT[b].reshape(D, T).T
    return out


# version 16 (final)
# speedup vs baseline: 1.2187x; 1.2187x over previous
import itertools
import numpy as np
import ml_dtypes
import concourse.bass as bass
import concourse.mybir as mybir
from concourse.bass_utils import run_bass_kernel_spmd
from concourse.alu_op_type import AluOpType
from contextlib import ExitStack

F32 = mybir.dt.float32
BF16 = mybir.dt.bfloat16
I32 = mybir.dt.int32
AF = mybir.ActivationFunctionType
AX = mybir.AxisListType

H = 16
DK = 128
M = 64
D = 2048
T = 2048
NB = 2
NCC = D // 128
NTT = T // 128
INTER = 5632
NIC = INTER // 128
ICB = 22
EPS_LN = 1e-5
EPS_RMS = 1e-5
CLAMP = 32.0
SCALE = float(DK) ** -0.5
ROPE_BASE = 10000.0
PI = float(np.pi)


class TK:
    ENGINES = ("pe", "dve", "act", "pool", "sp")
    Q2E = {"qsp": "sp", "qpool": "pool", "qact": "act", "qdve": "dve"}
    N_DMA_SEMS = 48

    def __init__(self, nc, stack):
        self.nc = nc
        self.sem = {}
        self.cnt = {}
        self.regions = {}
        self.actions = {e: [] for e in self.ENGINES}
        self.know = {e: {} for e in self.ENGINES}
        self.know_at = {}
        for s in self.ENGINES:
            self.sem[s] = stack.enter_context(nc.semaphore(f"sem_{s}"))
            self.cnt[s] = 0
        self.dma_sems = {"sw": [], "hw": []}
        for i in range(self.N_DMA_SEMS):
            cls = "sw" if i < self.N_DMA_SEMS // 2 else "hw"
            s = f"qd{i}"
            self.sem[s] = stack.enter_context(nc.semaphore(s))
            self.cnt[s] = 0
            self.dma_sems[cls].append(s)
        self.dma_rr = {"sw": 0, "hw": 0}

    def _wait(self, e, sem_name, val):
        if val <= 0:
            return
        if sem_name == e:
            if self.cnt[e] < val:
                raise RuntimeError(f"self-wait on future: {e} {val} > {self.cnt[e]}")
            return
        kn = self.know[e]
        if kn.get(sem_name, 0) >= val:
            return
        self.actions[e].append(("wait", sem_name, val))
        kn[sem_name] = val
        prod = self.know_at.get((sem_name, val))
        if prod:
            for s2, v2 in prod.items():
                if kn.get(s2, 0) < v2:
                    kn[s2] = v2

    def _wait_self(self, e, val):
        kn = self.know[e]
        if kn.get(e, 0) >= val:
            return
        self.actions[e].append(("wait", e, val))
        kn[e] = val
        prod = self.know_at.get((e, val))
        if prod:
            for s2, v2 in prod.items():
                if kn.get(s2, 0) < v2:
                    kn[s2] = v2

    def _collect(self, reads, writes):
        need = {}
        for k in reads:
            r = self.regions.get(k)
            if r and r["w"]:
                s, c = r["w"]
                need[s] = max(need.get(s, 0), c)
        for k in writes:
            r = self.regions.get(k)
            if r:
                if r["w"]:
                    s, c = r["w"]
                    need[s] = max(need.get(s, 0), c)
                for s, c in r["r"].items():
                    need[s] = max(need.get(s, 0), c)
        return need

    def _update(self, reads, writes, tag):
        s, c = tag
        for k in writes:
            self.regions[k] = {"w": tag, "r": {}}
        for k in reads:
            r = self.regions.setdefault(k, {"w": None, "r": {}})
            r["r"][s] = max(r["r"].get(s, 0), c)

    def _emit_waits(self, e, need):
        for s, c in sorted(need.items()):
            if s == e:
                if e != "pe":
                    self._wait_self(e, c)
            else:
                self._wait(e, s, c)

    def op(self, e, build, reads=(), writes=()):
        need = self._collect(reads, writes)
        self._emit_waits(e, need)
        self.cnt[e] += 1
        tag = (e, self.cnt[e])
        kn = dict(self.know[e])
        kn[e] = self.cnt[e]
        self.know_at[tag] = kn
        self.actions[e].append(("inst", build, e, 1, self.cnt[e]))
        self._update(reads, writes, tag)

    def dma(self, q, dst, src, reads=(), writes=(), **kw):
        e = self.Q2E[q]
        cls = "sw" if q == "qpool" else "hw"
        need = self._collect(reads, writes)
        pool = self.dma_sems[cls]
        s = pool[self.dma_rr[cls]]
        self.dma_rr[cls] = (self.dma_rr[cls] + 1) % len(pool)
        if self.cnt[s] > 0:
            need[s] = max(need.get(s, 0), self.cnt[s])
        self._emit_waits(e, need)
        self.cnt[s] += 16
        tag = (s, self.cnt[s])
        kn = dict(self.know[e])
        kn[s] = self.cnt[s]
        self.know_at[tag] = kn
        self.actions[e].append(("inst", lambda eng: eng.dma_start(dst, src, **kw),
                                s, 16, self.cnt[s]))
        self._update(reads, writes, tag)

    def fence_all(self, e="sp"):
        need = {s: c for s, c in self.cnt.items() if c > 0 and s != e}
        self._emit_waits(e, need)

    def fence(self):
        for e in self.ENGINES:
            need = {s: c for s, c in self.cnt.items() if c > 0 and s != e}
            self._emit_waits(e, need)

    def finalize_incs(self):
        needed = {}
        for e in self.ENGINES:
            for a in self.actions[e]:
                if a[0] == "wait":
                    needed.setdefault(a[1], set()).add(a[2])
        self.remap = {}
        for s in self.sem:
            vals = sorted(needed.get(s, ()))
            self.remap[s] = vals

    def _new_val(self, s, old):
        import bisect
        vals = self.remap[s]
        idx = bisect.bisect_right(vals, old)
        return idx

    def replay(self, e, eng):
        step = {s: (16 if s.startswith("qd") else 1) for s in self.sem}
        for a in self.actions[e]:
            if a[0] == "wait":
                _, s, old = a
                eng_newval = self._new_val(s, old) * step[s]
                eng.wait_ge(self.sem[s], eng_newval)
            else:
                _, build, s, amt, old = a
                inst = build(eng)
                if old in self.remap_set(s):
                    inst.then_inc(self.sem[s], amt)

    def remap_set(self, s):
        if not hasattr(self, "_rs"):
            self._rs = {k: set(v) for k, v in self.remap.items()}
        return self._rs.get(s, set())


def build_program(phases=("p0", "attn", "w", "ffn"), dbg=(), nheads=H, nbat=NB,
                  attn_upto=99):
    nc = bass.Bass("TRN2", target_bir_lowering=False)
    dt = {}

    def din(name, shape, dtyp=F32):
        dt[name] = nc.dram_tensor(name, shape, dtyp, kind="ExternalInput").ap()
        return dt[name]

    def dout(name, shape, dtyp=F32):
        dt[name] = nc.dram_tensor(name, shape, dtyp, kind="ExternalOutput").ap()
        return dt[name]

    xT = din("xT", [NB, NCC, 128, T])
    wq = din("wq", [H, 128, NCC * DK])
    wk = din("wk", [H, 128, NCC * DK])
    wv = din("wv", [H, 128, NCC * DK])
    wg = din("wg", [H, 128, NCC * DK])
    wo = din("wo", [H, 128, D])
    ws1t = din("ws1t", [128, NCC * 16])
    ws2b = din("ws2b", [17, H * M])
    gnw_in = din("gnw", [128, 1])
    gate = din("gate", [NIC, 128, NCC * 128])
    upw = din("upw", [NIC, 128, NCC * 128])
    down = din("down", [NCC, 128, NIC * 128])
    gs_in = din("gs", [128, NIC])
    us_in = din("us", [128, NIC])
    ds_in = din("ds", [128, NCC])
    outT = dout("outT", [NB, NCC, 128, T])
    ogt_s = nc.dram_tensor("ogt_s", [128, H, T], BF16).ap()
    part_s = nc.dram_tensor("part_s", [NCC, 128, T], F32).ap()

    with ExitStack() as st:
        def sb(name, shape, dtyp=F32):
            return st.enter_context(nc.sbuf_tensor(name, shape, dtyp))

        def psm(name, shape, dtyp=F32):
            return st.enter_context(nc.psum_tensor(name, shape, dtyp))

        tk = TK(nc, st)

        xnT = sb("xnT", [128, NCC * T], BF16)
        t1aug = sb("t1aug", [17, T], BF16)
        ws1_sb = sb("ws1_sb", [128, NCC * 16], BF16)
        ws2_sb = sb("ws2_sb", [17, H * M], BF16)
        gnw_sb = sb("gnw_sb", [128, 1], F32)
        gs_sb = sb("gs_sb", [128, NIC], F32)
        us_sb = sb("us_sb", [128, NIC], F32)
        ds_sb = sb("ds_sb", [128, NCC], F32)
        identb = sb("identb", [128, 128], BF16)
        identf64 = sb("identf64", [64, 64], F32)
        triu_m = sb("triu_m", [128, 128], F32)
        onescol = sb("onescol", [128, 1], BF16)
        ones_f = sb("ones_f", [1, 128], F32)
        negones_f = sb("negones_f", [1, 128], F32)
        phalf16 = sb("phalf16", [128, NTT], F32)
        cb32 = sb("cb32", [128, 1], F32)
        onesrow_b = sb("onesrow_b", [1, T], BF16)
        cb64 = sb("cb64", [128, 1], F32)
        ifcol = sb("ifcol", [64, 1], F32)
        cke_sb = [sb(f"cke{i}", [128, M], BF16) for i in range(2)]
        cev_sb = [sb(f"cev{i}", [64, 128], BF16) for i in range(2)]
        on_sb = [sb(f"on{i}", [128, 128], BF16) for i in range(4)]
        negmax_sb = sb("negmax_sb", [128, NTT], F32)
        rowsum_sb = sb("rowsum_sb", [128, NTT], F32)
        rcp16_sb = sb("rcp16_sb", [128, NTT], F32)
        ss_sb = sb("ss_sb", [128, NTT], F32)
        ms_sb = sb("ms_sb", [128, NTT], F32)
        rstd16 = sb("rstd16", [128, NTT], F32)

        ARB = sb("ARB", [128, 56832], BF16)
        ARF = sb("ARF", [128, 4096], F32)

        def vb(c0, n, p=128, p0=0):
            return ARB[p0:p0 + p, c0:c0 + n]

        def vf(c0, n, p=128, p0=0):
            return ARF[p0:p0 + p, c0:c0 + n]

        A_cos, A_sin = 0, 2048
        A_Pe = 4096
        A_OG = 5120
        A_qraw, A_qsw, A_ropw, A_ropz = 7168, 7680, 8192, 8704
        A_wq = [9216, 11264]
        A_wk = [13312, 15360]
        A_wv = [17408, 19456]
        A_wg = [21504, 23552]
        AH_BASE = [25600, 40960]
        AH_QT, AH_KT, AH_Ktok, AH_Vtok, AH_sGT = 0, 2048, 4096, 6144, 8192
        AH_es, AH_est, AH_PT = 10240, 11264, 13312
        P_xbf, P_xsq = 0, 2048
        W_wo, W_ogts, W_wp = 0, 32768, [49152, 50176]
        F_hdn = 0
        F_g = [45056, 47104]
        F_u = [49152, 51200]
        F_d = [45056, 49152]
        F_sg = 53248
        F_sg1 = 53760
        P_xf = [0, 2048]
        R_ang, R_tmp = 0, 2048
        A_cT = 0
        A_enz0 = 2048
        F_pin = [0, 512]
        F_xr = [1024, 1536]
        F_fo = [2048, 2560]
        F_f2 = [3072, 3584]

        psA = psm("psA", [128, 512], F32)
        psB = psm("psB", [128, 512], F32)
        psC = psm("psC", [128, 512], F32)
        psD = psm("psD", [128, 512], F32)
        psE = psm("psE", [128, 2048], F32)
        PS4 = [psA, psB, psC, psD]
        PSK = ["psA", "psB", "psC", "psD"]
        EK = ["psE0", "psE1", "psE2", "psE3"]

        def Ekeys(c0, c1):
            return [f"psE{i}" for i in range(4) if not (c1 <= i * 512 or c0 >= (i + 1) * 512)]

        block = st.enter_context(nc.Block())

        def emit_init():
            tk.dma("qpool", ws1_sb[:, :], ws1t, writes=["ws1"])
            tk.dma("qpool", ws2_sb[:, :], ws2b, writes=["ws2"])
            tk.dma("qsp", gnw_sb[:, :], gnw_in, writes=["gnw"])
            tk.dma("qsp", gs_sb[:, :], gs_in, writes=["gs"])
            tk.dma("qsp", us_sb[:, :], us_in, writes=["us"])
            tk.dma("qsp", ds_sb[:, :], ds_in, writes=["ds"])
            tk.op("pool", lambda g: g.memset(identb[:, :], 1.0), writes=["identb"])
            tk.op("pool", lambda g: g.affine_select(out=identb[:, :], in_=identb[:, :],
                  compare_op=AluOpType.is_equal, fill=0.0, base=0, pattern=[[-1, 128]],
                  channel_multiplier=1), reads=["identb"], writes=["identb"])
            tk.op("pool", lambda g: g.memset(identf64[:, :], 1.0), writes=["identf64"])
            tk.op("pool", lambda g: g.affine_select(out=identf64[:, :], in_=identf64[:, :],
                  compare_op=AluOpType.is_equal, fill=0.0, base=0, pattern=[[-1, 64]],
                  channel_multiplier=1), reads=["identf64"], writes=["identf64"])
            tk.op("pool", lambda g: g.memset(triu_m[:, :], 0.0), writes=["triu"])
            tk.op("pool", lambda g: g.affine_select(out=triu_m[:, :], in_=triu_m[:, :],
                  compare_op=AluOpType.is_gt, fill=1.0, base=0, pattern=[[-1, 128]],
                  channel_multiplier=1), reads=["triu"], writes=["triu"])
            tk.op("pool", lambda g: g.memset(onescol[:, :], 1.0), writes=["onescol"])
            tk.op("pool", lambda g: g.memset(ones_f[:, :], 1.0), writes=["ones_f"])
            tk.op("pool", lambda g: g.memset(negones_f[:, :], -1.0), writes=["negones_f"])
            tk.op("pool", lambda g: g.memset(phalf16[:, :], -0.5), writes=["phalf16"])
            tk.op("pool", lambda g: g.memset(cb32[:, :], 32.0), writes=["cb"])
            tk.op("pool", lambda g: g.memset(cb64[:, :], 64.0), writes=["cb"])
            tk.op("pool", lambda g: g.memset(onesrow_b[:, :], 1.0), writes=["onesrow"])
            tk.dma("qsp", t1aug[16:17, :], onesrow_b[:, :], reads=["onesrow"],
                   writes=["t1aug16"])
            tk.op("pool", lambda g: g.iota(ifcol[:, :], [[1, 1]], base=0,
                  channel_multiplier=1, allow_small_or_imprecise_dtypes=True), writes=["ifcol"])
            tk.op("act", lambda s: s.activation(ifcol[:, :], ifcol[:, :], AF.Exp,
                  scale=-float(np.log(ROPE_BASE)) / 64.0), reads=["ifcol"], writes=["ifcol"])

        def emit_ropetab():
            ang = vf(R_ang, 2048, p=64)
            tmp = vf(R_tmp, 2048, p=64)
            angI = ARF[0:64, R_tmp:R_tmp + 2048].bitcast(I32)
            tk.op("pool", lambda g: g.iota(ang, [[1, T]], base=0, channel_multiplier=0, allow_small_or_imprecise_dtypes=True),
                  writes=["Rang"])
            tk.op("dve", lambda v: v.tensor_scalar(ang, ang, ifcol[:, :], None,
                  AluOpType.mult), reads=["Rang", "ifcol"], writes=["Rang"])
            tk.op("dve", lambda v: v.tensor_scalar(angI, ang, 1.0 / (2 * PI), None,
                  AluOpType.mult), reads=["Rang"], writes=["Rtmp"])
            tk.op("dve", lambda v: v.scalar_tensor_tensor(ang, angI, -2 * PI, ang,
                  AluOpType.mult, AluOpType.add), reads=["Rtmp", "Rang"], writes=["Rang"])
            tk.op("dve", lambda v: v.tensor_scalar(tmp, ang, PI, None, AluOpType.is_gt),
                  reads=["Rang"], writes=["Rtmp"])
            tk.op("dve", lambda v: v.scalar_tensor_tensor(ang, tmp, -2 * PI, ang,
                  AluOpType.mult, AluOpType.add), reads=["Rtmp", "Rang"], writes=["Rang"])
            tk.op("dve", lambda v: v.tensor_scalar(tmp, ang, -PI, None, AluOpType.is_lt),
                  reads=["Rang"], writes=["Rtmp"])
            tk.op("dve", lambda v: v.scalar_tensor_tensor(ang, tmp, 2 * PI, ang,
                  AluOpType.mult, AluOpType.add), reads=["Rtmp", "Rang"], writes=["Rang"])
            tk.op("act", lambda s: s.activation(tmp, ang, AF.Sin), reads=["Rang"],
                  writes=["Rtmp"])
            tk.op("dve", lambda v: v.tensor_copy(ARB[64:128, A_sin:A_sin + 2048], tmp),
                  reads=["Rtmp"], writes=["sinT"])
            tk.op("dve", lambda v: v.tensor_scalar(tmp, tmp, -1.0, None,
                  AluOpType.mult), reads=["Rtmp"], writes=["Rtmp"])
            tk.op("dve", lambda v: v.tensor_copy(vb(A_sin, 2048, p=64), tmp),
                  reads=["Rtmp"], writes=["sinT"])
            tk.op("dve", lambda v: v.tensor_scalar(ang, ang, PI / 2, None, AluOpType.add),
                  reads=["Rang"], writes=["Rang"])
            tk.op("dve", lambda v: v.tensor_scalar(tmp, ang, PI, None, AluOpType.is_gt),
                  reads=["Rang"], writes=["Rtmp"])
            tk.op("dve", lambda v: v.scalar_tensor_tensor(ang, tmp, -2 * PI, ang,
                  AluOpType.mult, AluOpType.add), reads=["Rtmp", "Rang"], writes=["Rang"])
            tk.op("act", lambda s: s.activation(ang, ang, AF.Sin), reads=["Rang"],
                  writes=["Rang"])
            tk.op("dve", lambda v: v.tensor_copy(vb(A_cos, 2048, p=64), ang),
                  reads=["Rang"], writes=["cosT"])
            tk.op("dve", lambda v: v.tensor_copy(ARB[64:128, A_cos:A_cos + 2048], ang),
                  reads=["Rang"], writes=["cosT"])

        def emit_p0(b):
            xbf = vb(P_xbf, 2048)
            xsq = vb(P_xsq, 2048)
            rowA = xnT[0:1, 0:4096].bitcast(F32)
            rowB = xnT[0:1, 4096:8192].bitcast(F32)
            rowC = xnT[0:1, 8192:12288].bitcast(F32)
            for cc in range(NCC):
                bi = cc % 2
                xfv = vf(P_xf[bi], 2048)
                tk.dma("qsp" if bi == 0 else "qact", xfv, xT[b, cc], writes=[f"xf{bi}"])
                tk.op("dve", lambda v, xfv=xfv: v.tensor_copy(xbf, xfv),
                      reads=[f"xf{bi}"], writes=["xbf"])
                tk.op("dve", lambda v: v.tensor_tensor(xsq, xbf, xbf,
                      AluOpType.mult), reads=["xbf"], writes=["xsq"])
                for sl in range(4):
                    c0, c1 = sl * 512, (sl + 1) * 512
                    tk.op("pe", lambda t, c0=c0, c1=c1, cc=cc:
                          t.matmul(psE[0:1, c0:c1], onescol[:, :], xbf[:, c0:c1],
                                   start=(cc == 0), stop=(cc == NCC - 1),
                                   skip_group_check=True),
                          reads=["onescol", "xbf"], writes=Ekeys(c0, c1))
                    tk.op("pe", lambda t, sl=sl, c0=c0, c1=c1, cc=cc:
                          t.matmul(PS4[sl][0:1, :], onescol[:, :], xsq[:, c0:c1],
                                   start=(cc == 0), stop=(cc == NCC - 1),
                                   skip_group_check=True),
                          reads=["onescol", "xsq"], writes=[PSK[sl]])
            tk.op("dve", lambda v: v.tensor_copy(rowA, psE[0:1, :]),
                  reads=EK, writes=["xnT"])
            for sl in range(4):
                tk.op("dve", lambda v, sl=sl: v.tensor_copy(
                      rowB[0:1, sl * 512:(sl + 1) * 512], PS4[sl][0:1, :]),
                      reads=[PSK[sl]], writes=["xnT"])
            tk.op("dve", lambda v: v.tensor_scalar(rowA, rowA, 1.0 / D, None,
                  AluOpType.mult), reads=["xnT"], writes=["xnT"])
            tk.op("dve", lambda v: v.tensor_tensor(rowC, rowA, rowA,
                  AluOpType.mult), reads=["xnT"], writes=["xnT"])
            tk.op("dve", lambda v: v.tensor_scalar(rowB, rowB, 1.0 / D,
                  EPS_LN, AluOpType.mult, AluOpType.add), reads=["xnT"], writes=["xnT"])
            tk.op("dve", lambda v: v.tensor_tensor(rowB, rowB, rowC,
                  AluOpType.subtract), reads=["xnT"], writes=["xnT"])
            tk.op("act", lambda s: s.activation(rowB, rowB, AF.Ln),
                  reads=["xnT"], writes=["xnT"])
            tk.op("act", lambda s: s.activation(rowB, rowB, AF.Exp, scale=-0.5),
                  reads=["xnT"], writes=["xnT"])
            for sl in range(4):
                c0, c1 = sl * 512, (sl + 1) * 512
                tk.op("pe", lambda t, sl=sl, c0=c0, c1=c1: t.matmul(PS4[sl][:, :],
                      ones_f[:, :], rowB[0:1, c0:c1], start=True, stop=True),
                      reads=["ones_f", "xnT"], writes=[PSK[sl]])
            tk.op("dve", lambda v: v.tensor_tensor(rowC, rowA, rowB,
                  AluOpType.mult), reads=["xnT"], writes=["xnT"])
            for sl in range(4):
                c0, c1 = sl * 512, (sl + 1) * 512
                tk.op("pe", lambda t, c0=c0, c1=c1: t.matmul(psE[:, c0:c1], negones_f[:, :],
                      rowC[0:1, c0:c1], start=True, stop=True),
                      reads=["negones_f", "xnT"], writes=Ekeys(c0, c1))
            for cc in range(NCC):
                bi = cc % 2
                xfv = vf(P_xf[bi], 2048)
                tk.dma("qsp" if bi == 0 else "qact", xfv, xT[b, cc], writes=[f"xf{bi}"])
                for sl in range(4):
                    c0, c1 = sl * 512, (sl + 1) * 512
                    tk.op("dve", lambda v, xfv=xfv, c0=c0, c1=c1, sl=sl:
                          v.tensor_tensor(xfv[:, c0:c1], xfv[:, c0:c1], PS4[sl][:, :],
                          AluOpType.mult), reads=[f"xf{bi}", PSK[sl]], writes=[f"xf{bi}"])
                tk.op("dve", lambda v, cc=cc, xfv=xfv: v.tensor_tensor(
                      xnT[:, cc * T:(cc + 1) * T], xfv[:, :], psE[:, :], AluOpType.add),
                      reads=[f"xf{bi}"] + EK, writes=["xnT"])
            for sl in range(4):
                c0, c1 = sl * 512, (sl + 1) * 512
                for cc in range(NCC):
                    tk.op("pe", lambda t, cc=cc, c0=c0, c1=c1:
                          t.matmul(psE[0:16, c0:c1], ws1_sb[:, cc * 16:(cc + 1) * 16],
                                   xnT[:, cc * T + c0:cc * T + c1],
                                   start=(cc == 0), stop=(cc == NCC - 1),
                                   skip_group_check=True),
                          reads=["ws1", "xnT"], writes=Ekeys(c0, c1))
            tk.op("dve", lambda v: v.tensor_copy(t1aug[0:16, :], psE[0:16, :]),
                  reads=EK, writes=["t1aug"])

        def prefetch_head(h):
            p = h % 2
            for (arr, off, key) in ((wq, A_wq[p], f"wq{p}"), (wk, A_wk[p], f"wk{p}"),
                                    (wv, A_wv[p], f"wv{p}"), (wg, A_wg[p], f"wg{p}")):
                tk.dma("qpool", vb(off, 2048), arr[h], writes=[key])

        def hv(h):
            p = h % 2
            base = AH_BASE[p]
            return dict(
                p=p,
                QT=vb(base + AH_QT, 2048), KT=vb(base + AH_KT, 2048),
                Ktok=vb(base + AH_Ktok, 2048), Vtok=vb(base + AH_Vtok, 2048),
                sGT=vb(base + AH_sGT, 2048), esv=vb(base + AH_es, 1024),
                est=vb(base + AH_est, 2048, p=64), PT=vb(base + AH_PT, 2048, p=64),
                kQT=f"QT{p}", kKT=f"KT{p}", kKtok=f"Ktok{p}", kVtok=f"Vtok{p}",
                ksGT=f"sGT{p}", kes=f"es{p}", kest=f"est{p}", kPT=f"PT{p}",
                enz=vf(A_enz0 + p * 1024, 1024), kenz=f"enz{p}",
                tok=vf(A_enz0 + (1 - p) * 1024, 1024), ktok=f"enz{1 - p}",
            )

        def s1_gen(b, h):
            v = hv(h)
            p = v["p"]
            if h + 1 < H:
                prefetch_head(h + 1)
                yield
            wqv = vb(A_wq[p], 2048)
            wkv = vb(A_wk[p], 2048)
            wvv = vb(A_wv[p], 2048)
            wgv = vb(A_wg[p], 2048)
            qraw = vb(A_qraw, 512)
            qsw = vb(A_qsw, 512)
            ropw = vb(A_ropw, 512)
            ropz = vb(A_ropz, 512)
            PSF = [psA, psB]
            KSF = ["psA", "psB"]
            esv = v["esv"]
            for w in range(2):
                ps = PSF[w]
                kk = KSF[w]
                po = PSF[1 - w]
                ko = KSF[1 - w]
                for t8 in range(8):
                    tt = 8 * w + t8
                    tk.op("pe", lambda t, ps=ps, t8=t8, tt=tt: t.matmul(
                          ps[:, t8 * 64:(t8 + 1) * 64],
                          t1aug[:, tt * 128:(tt + 1) * 128],
                          ws2_sb[:, h * M:(h + 1) * M],
                          start=(t8 == 0), stop=(t8 == 7), skip_group_check=True),
                          reads=["t1aug", "t1aug16", "ws2"], writes=[kk])
                    if t8 % 4 == 3:
                        yield
                tk.op("act", lambda s, ps=ps, po=po: s.activation(po[:, :], ps[:, :],
                      AF.Relu, bias=cb32[:, :]), reads=[kk, "cb"], writes=[ko])
                yield
                tk.op("act", lambda s, ps=ps, po=po: s.activation(ps[:, :], po[:, :],
                      AF.Relu, scale=-1.0, bias=cb64[:, :]), reads=[ko, "cb"],
                      writes=[kk])
                yield
                tk.op("act", lambda s, ps=ps, w=w: s.activation(
                      esv[:, w * 512:(w + 1) * 512], ps[:, :], AF.Exp,
                      scale=-1.0, bias=cb32[:, :]), reads=[kk, "cb"], writes=[v["kes"]])
                yield
            for w in range(2):
                ps = PSF[w]
                kk = KSF[w]
                for t8 in range(8):
                    tt = 8 * w + t8
                    ov = ps[0:64, t8 * 64:(t8 + 1) * 64].bitcast(BF16)
                    tk.op("pe", lambda t, ov=ov, tt=tt, esv=esv: t.transpose(ov,
                          esv[:, tt * 64:(tt + 1) * 64], identb[:, :]),
                          reads=[v["kes"], "identb"], writes=[kk])
                    if t8 % 4 == 3:
                        yield
                tk.op("dve", lambda v_, ps=ps, w=w: v_.tensor_copy(
                      v["est"][:, w * 1024:(w + 1) * 1024], ps[0:64, :].bitcast(BF16)),
                      reads=[kk], writes=[v["kest"]])
                yield
            cTv = vf(A_cT, 2048, p=64)
            for q in range(4):
                c0 = q * 512
                init = 0.0 if q == 0 else cTv[:, c0 - 1:c0]
                tk.op("dve", lambda v_, v=v, c0=c0, init=init: v_.tensor_tensor_scan(
                      cTv[:, c0:c0 + 512], v["est"][:, c0:c0 + 512],
                      v["est"][:, c0:c0 + 512], init, AluOpType.add, AluOpType.bypass),
                      reads=[v["kest"], "cT"], writes=["cT"])
                yield
            for (wv_, out, wkey, okey) in ((wqv, v["QT"], f"wq{p}", v["kQT"]),
                                           (wkv, v["KT"], f"wk{p}", v["kKT"])):
                for sl in range(4):
                    c0 = sl * 512
                    ps = PSF[sl % 2]
                    kk = KSF[sl % 2]
                    for cc in range(NCC):
                        tk.op("pe", lambda t, ps=ps, wv_=wv_, cc=cc, c0=c0:
                              t.matmul(ps[:, :], wv_[:, cc * DK:cc * DK + 128],
                                       xnT[:, cc * T + c0:cc * T + c0 + 512],
                                       start=(cc == 0), stop=(cc == NCC - 1)),
                              reads=[wkey, "xnT"], writes=[kk])
                        if cc % 4 == 3:
                            yield
                    tk.op("act", lambda s, ps=ps: s.activation(qraw[:, :], ps[:, :],
                          AF.Copy), reads=[kk], writes=["qraw"])
                    yield
                    tk.op("dve", lambda v_: v_.tensor_copy(qsw[0:64, :],
                          qraw[64:128, :]), reads=["qraw"], writes=["qsw"])
                    tk.op("dve", lambda v_: v_.tensor_copy(qsw[64:128, :],
                          qraw[0:64, :]), reads=["qraw"], writes=["qsw"])
                    yield
                    tk.op("dve", lambda v_, c0=c0: v_.tensor_tensor(ropw[:, :],
                          qraw[:, :], vb(A_cos + c0, 512), AluOpType.mult),
                          reads=["qraw", "cosT"], writes=["ropw"])
                    tk.op("pool", lambda g, c0=c0: g.tensor_tensor(ropz[:, :], qsw[:, :],
                          vb(A_sin + c0, 512), AluOpType.mult),
                          reads=["qsw", "sinT"], writes=["ropz"])
                    yield
                    tk.op("dve", lambda v_, out=out, c0=c0: v_.tensor_tensor(
                          out[:, c0:c0 + 512], ropw[:, :], ropz[:, :], AluOpType.add),
                          reads=["ropw", "ropz"], writes=[okey])
                    yield
            ktokreg = v["Ktok"]
            for sl in range(4):
                c0 = sl * 512
                ps = PSF[sl % 2]
                kk = KSF[sl % 2]
                for cc in range(NCC):
                    tk.op("pe", lambda t, ps=ps, cc=cc, c0=c0, wvv=wvv:
                          t.matmul(ps[:, :], wvv[:, cc * DK:cc * DK + 128],
                                   xnT[:, cc * T + c0:cc * T + c0 + 512],
                                   start=(cc == 0), stop=(cc == NCC - 1)),
                          reads=[f"wv{p}", "xnT"], writes=[kk])
                    if cc % 4 == 3:
                        yield
                tk.op("act", lambda s, ps=ps, c0=c0: s.activation(
                      ktokreg[:, c0:c0 + 512], ps[:, :], AF.Copy),
                      reads=[kk], writes=[v["kKtok"]])
                yield
            for tt in range(NTT):
                ps = PSF[tt % 2]
                kk = KSF[tt % 2]
                pview = ps[:, 0:64].bitcast(BF16)
                tk.op("pe", lambda t, pview=pview, tt=tt: t.transpose(pview,
                      ktokreg[:, tt * 128:(tt + 1) * 128], identb[:, :]),
                      reads=[v["kKtok"], "identb"], writes=[kk])
                tk.op("act", lambda s, pview=pview, tt=tt: s.activation(
                      v["Vtok"][:, tt * 128:(tt + 1) * 128], pview, AF.Copy),
                      reads=[kk], writes=[v["kVtok"]])
                if tt % 2 == 1:
                    yield
            for sl in range(4):
                c0 = sl * 512
                ps = PSF[sl % 2]
                kk = KSF[sl % 2]
                for cc in range(NCC):
                    tk.op("pe", lambda t, ps=ps, cc=cc, c0=c0, wgv=wgv:
                          t.matmul(ps[:, :], wgv[:, cc * DK:cc * DK + 128],
                                   xnT[:, cc * T + c0:cc * T + c0 + 512],
                                   start=(cc == 0), stop=(cc == NCC - 1)),
                          reads=[f"wg{p}", "xnT"], writes=[kk])
                    if cc % 4 == 3:
                        yield
                tk.op("act", lambda s, ps=ps: s.activation(ropw[:, :], ps[:, :], AF.Tanh,
                      scale=0.5), reads=[kk], writes=["ropw"])
                yield
                tk.op("dve", lambda v_: v_.tensor_scalar(ropz[:, :], ropw[:, :], 0.5, 0.5,
                      AluOpType.mult, AluOpType.add), reads=["ropw"], writes=["ropz"])
                yield
                tk.op("dve", lambda v_, ps=ps, c0=c0, kk=kk: v_.tensor_tensor(
                      v["sGT"][:, c0:c0 + 512], ropz[:, :], ps[:, :], AluOpType.mult),
                      reads=["ropz", kk], writes=[v["ksGT"]])
                yield
            for tt in range(NTT):
                ps = PSF[tt % 2]
                kk = KSF[tt % 2]
                pview = ps[:, 0:64].bitcast(BF16)
                tk.op("pe", lambda t, pview=pview, tt=tt: t.transpose(pview,
                      v["KT"][:, tt * 128:(tt + 1) * 128], identb[:, :]),
                      reads=[v["kKT"], "identb"], writes=[kk])
                tk.op("act", lambda s, pview=pview, tt=tt: s.activation(
                      ktokreg[:, tt * 128:(tt + 1) * 128], pview, AF.Copy),
                      reads=[kk], writes=[v["kKtok"]])
                if tt % 2 == 1:
                    yield
            yield "enzbar"
            for w in range(2):
                ps = PSF[w]
                kk = KSF[w]
                for t8 in range(8):
                    tt = 8 * w + t8
                    tk.op("pe", lambda t, ps=ps, t8=t8, tt=tt: t.transpose(
                          ps[:, t8 * 64:(t8 + 1) * 64],
                          cTv[:, tt * 128:(tt + 1) * 128], identf64[:, :]),
                          reads=["cT", "identf64"], writes=[kk])
                    if t8 % 4 == 3:
                        yield
                lntmp = vf(A_cT, 512)
                tk.op("act", lambda s, ps=ps, lntmp=lntmp: s.activation(lntmp, ps[:, :],
                      AF.Ln), reads=[kk], writes=["cT"])
                yield
                tk.op("act", lambda s, w=w, v=v, lntmp=lntmp: s.activation(
                      v["enz"][:, w * 512:(w + 1) * 512], lntmp, AF.Exp, scale=-1.0),
                      reads=["cT"], writes=[v["kenz"]])
                yield

        def s2a(b, h, pump):
            v = hv(h)
            esv = v["esv"]
            for w in range(4):
                atm4 = vb(A_Pe + (w % 2) * 512, 512)
                for k in range(4):
                    i = 4 * w + k
                    tk.op("pe", lambda t, k=k, i=i, v=v: t.matmul(
                          psD[:, k * 128:(k + 1) * 128],
                          v["KT"][:, i * 128:(i + 1) * 128],
                          v["QT"][:, i * 128:(i + 1) * 128],
                          start=(k == 0), stop=(k == 3), skip_group_check=True),
                          reads=[v["kKT"], v["kQT"]], writes=["psD"])
                pump(2)
                d3 = psD[:, :].rearrange("p (k f) -> p k f", f=128)
                a3 = atm4.rearrange("p (k f) -> p k f", f=128)
                tk.op("dve", lambda v_, a3=a3, d3=d3: v_.scalar_tensor_tensor(
                      a3, d3, SCALE,
                      triu_m[:, :].unsqueeze(1).broadcast_to([128, 4, 128]),
                      AluOpType.mult, AluOpType.mult),
                      reads=["psD", "triu"], writes=["Pe"])
                pump(2)
                for k in range(4):
                    i = 4 * w + k
                    okv = psE[:, i * 64:(i + 1) * 64]
                    okk = Ekeys(i * 64, (i + 1) * 64)
                    if i > 0:
                        tk.op("pe", lambda t, okv=okv, i=i, v=v: t.matmul(okv,
                              v["QT"][:, i * 128:(i + 1) * 128],
                              cke_sb[(i - 1) % 2][:, :],
                              start=(i == 8), stop=False, skip_group_check=True),
                              reads=[v["kQT"], f"cke{(i - 1) % 2}"], writes=okk)
                    tk.op("pe", lambda t, okv=okv, i=i, k=k, atm4=atm4, esv=esv:
                          t.matmul(okv, atm4[:, k * 128:(k + 1) * 128],
                                   esv[:, i * 64:(i + 1) * 64],
                                   start=(i == 0), stop=True, skip_group_check=True),
                          reads=["Pe", v["kes"]], writes=okk)
                    if i + 1 < NTT:
                        tk.op("pe", lambda t, i=i, v=v, esv=esv: t.matmul(psC[:, 0:64],
                              v["Ktok"][:, i * 128:(i + 1) * 128],
                              esv[:, i * 64:(i + 1) * 64],
                              start=(i == 0), stop=(i == NTT - 2), skip_group_check=True),
                              reads=[v["kKtok"], v["kes"]], writes=["psC"])
                        tk.op("dve", lambda v_, i=i: v_.tensor_scalar(
                              cke_sb[i % 2][:, :], psC[:, 0:64], SCALE, None,
                              AluOpType.mult), reads=["psC"], writes=[f"cke{i % 2}"])
                    pump(2)

        def s2b(b, h, pump):
            v = hv(h)
            enzv = v["enz"]
            tokv = v["tok"]
            ktok = v["ktok"]
            Pev = vb(A_Pe, 1024)
            tok3 = tokv.rearrange("p (i m) -> p i m", m=M)
            e3 = Pev.rearrange("p (i m) -> p i m", m=M)
            tk.op("dve", lambda v_: v_.tensor_tensor(tokv, psE[:, 0:1024], enzv,
                  AluOpType.mult), reads=["psE0", "psE1", v["kenz"]], writes=[ktok])
            pump(3)
            tk.op("dve", lambda v_: v_.tensor_reduce(negmax_sb[:, :], tok3, axis=AX.X,
                  op=AluOpType.max, negate=True), reads=[ktok], writes=["negmax"])
            pump(3)
            tk.op("dve", lambda v_: v_.tensor_tensor(tok3, tok3,
                  negmax_sb[:, :].unsqueeze(2).broadcast_to([128, NTT, M]),
                  AluOpType.add), reads=[ktok, "negmax"], writes=[ktok])
            pump(3)
            tk.op("act", lambda s: s.activation(Pev, tokv, AF.Exp), reads=[ktok],
                  writes=["Pe"])
            pump(3)
            tk.op("dve", lambda v_: v_.tensor_reduce(rowsum_sb[:, :], e3, axis=AX.X,
                  op=AluOpType.add), reads=["Pe"], writes=["rowsum"])
            pump(3)
            tk.op("dve", lambda v_: v_.reciprocal(rcp16_sb[:, :], rowsum_sb[:, :]),
                  reads=["rowsum"], writes=["rcp16"])
            pump(3)
            tk.op("dve", lambda v_: v_.tensor_tensor(tok3, e3,
                  rcp16_sb[:, :].unsqueeze(2).broadcast_to([128, NTT, M]),
                  AluOpType.mult), reads=["Pe", "rcp16"], writes=[ktok])
            pump(3)
            tk.op("dve", lambda v_: v_.tensor_tensor(Pev, tokv, enzv, AluOpType.mult),
                  reads=[ktok, v["kenz"]], writes=["Pe"])
            pump(3)
            for tt in range(NTT):
                ov = psE[0:64, tt * 64:(tt + 1) * 64].bitcast(BF16)
                tk.op("pe", lambda t, ov=ov, tt=tt, Pev=Pev: t.transpose(ov,
                      Pev[:, tt * 64:(tt + 1) * 64], identb[:, :]),
                      reads=["Pe", "identb"], writes=Ekeys(tt * 64, (tt + 1) * 64))
                if tt % 4 == 3:
                    pump(2)
            tk.op("dve", lambda v_, v=v: v_.tensor_copy(v["PT"],
                  psE[0:64, 0:1024].bitcast(BF16)), reads=["psE0", "psE1"],
                  writes=[v["kPT"]])
            pump(2)

        def s3(b, h, pump):
            v = hv(h)
            esv = v["esv"]
            for w in range(4):
                w2m4 = vb(A_Pe + (w % 2) * 512, 512)
                for k in range(4):
                    i = 4 * w + k
                    tk.op("pe", lambda t, k=k, i=i, v=v: t.matmul(
                          psD[:, k * 128:(k + 1) * 128],
                          v["est"][:, i * 128:(i + 1) * 128],
                          v["PT"][:, i * 128:(i + 1) * 128],
                          start=(k == 0), stop=(k == 3), skip_group_check=True),
                          reads=[v["kest"], v["kPT"]], writes=["psD"])
                pump(2)
                d3 = psD[:, :].rearrange("p (k f) -> p k f", f=128)
                a3 = w2m4.rearrange("p (k f) -> p k f", f=128)
                tk.op("dve", lambda v_, a3=a3, d3=d3: v_.tensor_tensor(
                      a3, d3, triu_m[:, :].unsqueeze(1).broadcast_to([128, 4, 128]),
                      AluOpType.mult), reads=["psD", "triu"], writes=["Pe"])
                pump(2)
                for k in range(4):
                    i = 4 * w + k
                    ov = psE[:, i * 128:(i + 1) * 128]
                    okk = Ekeys(i * 128, (i + 1) * 128)
                    if i > 0:
                        tk.op("pe", lambda t, ov=ov, i=i, v=v: t.matmul(ov,
                              v["PT"][:, i * 128:(i + 1) * 128],
                              cev_sb[(i - 1) % 2][:, :],
                              start=(i % 4 == 0), stop=False, skip_group_check=True),
                              reads=[v["kPT"], f"cev{(i - 1) % 2}"], writes=okk)
                    tk.op("pe", lambda t, ov=ov, i=i, k=k, w2m4=w2m4, v=v:
                          t.matmul(ov, w2m4[:, k * 128:(k + 1) * 128],
                                   v["Vtok"][:, i * 128:(i + 1) * 128],
                                   start=(i == 0), stop=True, skip_group_check=True),
                          reads=["Pe", v["kVtok"]], writes=okk)
                    if i + 1 < NTT:
                        tk.op("pe", lambda t, i=i, v=v, esv=esv: t.matmul(
                              psC[0:64, 0:128], esv[:, i * 64:(i + 1) * 64],
                              v["Vtok"][:, i * 128:(i + 1) * 128],
                              start=(i == 0), stop=(i == NTT - 2), skip_group_check=True),
                              reads=[v["kes"], v["kVtok"]], writes=["psC"])
                        tk.op("dve", lambda v_, i=i: v_.tensor_copy(cev_sb[i % 2][:, :],
                              psC[0:64, 0:128]), reads=["psC"], writes=[f"cev{i % 2}"])
                    pump(2)

        def s4a(b, h, pump):
            v = hv(h)
            OG = vb(A_OG, 2048)
            Pev = vb(A_Pe, 1024)
            tk.op("act", lambda s: s.activation(OG[:, :], psE[:, :], AF.Copy),
                  reads=EK, writes=["OG"])
            pump(2)
            for w in range(2):
                tk.op("dve", lambda v_, w=w: v_.tensor_tensor(Pev,
                      OG[:, w * 1024:(w + 1) * 1024], OG[:, w * 1024:(w + 1) * 1024],
                      AluOpType.mult), reads=["OG"], writes=["Pe"])
                pump(2)
                tk.op("dve", lambda v_, w=w: v_.tensor_reduce(ss_sb[:, w * 8:(w + 1) * 8],
                      Pev.rearrange("p (i f) -> p i f", f=128), axis=AX.X,
                      op=AluOpType.add), reads=["Pe"], writes=["ss"])
                pump(2)
            tk.op("dve", lambda v_: v_.tensor_scalar(ms_sb[:, :], ss_sb[:, :], 1.0 / 128,
                  EPS_RMS, AluOpType.mult, AluOpType.add), reads=["ss"], writes=["ms"])
            tk.op("pool", lambda g: g.tensor_tensor(rstd16[:, :], ms_sb[:, :],
                  phalf16[:, :], AluOpType.pow), reads=["ms", "phalf16"],
                  writes=["rstd16"])
            pump(2)
            for i in range(NTT):
                tk.op("dve", lambda v_, i=i: v_.tensor_scalar(
                      OG[:, i * 128:(i + 1) * 128], psE[:, i * 128:(i + 1) * 128],
                      rstd16[:, i:i + 1], None, AluOpType.mult),
                      reads=Ekeys(i * 128, (i + 1) * 128) + ["rstd16", "OG"],
                      writes=["OG"])
                pump(1)

        def s4b_gen(b, h, dump=None):
            v = hv(h)
            OG = vb(A_OG, 2048)
            for hw in range(2):
                base = 1024 + hw * 512
                for k in range(8):
                    i = hw * 8 + k
                    ov = psE[:, base + k * 64:base + (k + 1) * 64].bitcast(BF16)
                    tk.op("pe", lambda t, ov=ov, i=i: t.transpose(ov,
                          OG[:, i * 128:(i + 1) * 128], identb[:, :]),
                          reads=["OG", "identb"],
                          writes=Ekeys(base + k * 64, base + (k + 1) * 64))
                    if k % 2 == 1:
                        yield
                tk.op("dve", lambda v_, hw=hw, base=base, v=v: v_.scalar_tensor_tensor(
                      OG[:, hw * 1024:(hw + 1) * 1024],
                      psE[:, base:base + 512].bitcast(BF16),
                      gnw_sb[:, :], v["sGT"][:, hw * 1024:(hw + 1) * 1024],
                      AluOpType.mult, AluOpType.mult),
                      reads=Ekeys(base, base + 512) + ["gnw", v["ksGT"], "OG"],
                      writes=["OG"])
                yield
            tk.dma("qsp", ogt_s[:, h, :], OG[:, :], reads=["OG"], writes=[f"ogt{h}"])
            if dump:
                for name, ap, keys in dump:
                    if ap is not None:
                        tk.dma("qsp", dt[name], ap, reads=keys)
            yield

        def emit_w(b):
            wov = vb(W_wo, 32768)
            OGB = [W_ogts, W_ogts + 8192]
            for hc in range(H):
                dst = vb(W_wo + hc * 2048, 2048)
                tk.dma("qpool", dst, wo[hc], writes=[f"wo{hc}"])
            dst0 = vb(OGB[0], 8192).rearrange("p (h c) -> p h c", c=512)
            tk.dma("qsp", dst0, ogt_s[:, :, 0:512],
                   reads=[f"ogt{hh}" for hh in range(H)], writes=["ogts0"])
            for ts in range(4):
                ogts = vb(OGB[ts % 2], 8192)
                if ts + 1 < 4:
                    dstn = vb(OGB[(ts + 1) % 2], 8192).rearrange("p (h c) -> p h c", c=512)
                    tk.dma("qsp", dstn, ogt_s[:, :, (ts + 1) * 512:(ts + 2) * 512],
                           reads=[f"ogt{hh}" for hh in range(H)],
                           writes=[f"ogts{(ts + 1) % 2}"])
                for dtt in range(NCC):
                    ps = PS4[dtt % 4]
                    for hc in range(H):
                        tk.op("pe", lambda t, ps=ps, hc=hc, dtt=dtt, ogts=ogts:
                              t.matmul(ps[:, :],
                              wov[:, hc * 2048 + dtt * 128:hc * 2048 + (dtt + 1) * 128],
                              ogts[:, hc * 512:(hc + 1) * 512],
                              start=(hc == 0), stop=(hc == H - 1)),
                              reads=[f"wo{hc}", f"ogts{ts % 2}"], writes=[PSK[dtt % 4]])
                    wp = ARB[:, W_wp[dtt % 2]:W_wp[dtt % 2] + 1024].bitcast(F32)
                    tk.op("dve", lambda v, ps=ps, wp=wp, dtt=dtt: v.tensor_copy(wp, ps[:, :]),
                          reads=[PSK[dtt % 4]], writes=[f"wpart{dtt % 2}"])
                    tk.dma("qsp", part_s[dtt, :, ts * 512:(ts + 1) * 512], wp,
                           reads=[f"wpart{dtt % 2}"], writes=[f"part{dtt}_{ts}"])

        def emit_f(b, r):
            ic0 = r * ICB
            sgv = vb(F_sg, 512)
            sg2 = [vb(F_sg, 512), vb(F_sg1, 512)]
            tk.dma("qpool", vb(F_g[0], 2048), gate[ic0], writes=["gch0"])
            tk.dma("qpool", vb(F_u[0], 2048), upw[ic0], writes=["uch0"])
            for k in range(ICB):
                ic = ic0 + k
                p = k % 2
                gv = vb(F_g[p], 2048)
                uv = vb(F_u[p], 2048)
                if k + 1 < ICB:
                    tk.dma("qpool", vb(F_g[1 - p], 2048), gate[ic + 1],
                           writes=[f"gch{1 - p}"])
                    tk.dma("qpool", vb(F_u[1 - p], 2048), upw[ic + 1],
                           writes=[f"uch{1 - p}"])
                for sl in range(4):
                    c0 = sl * 512
                    psg = [psA, psC][sl % 2]
                    psu = [psB, psD][sl % 2]
                    kg = ["psA", "psC"][sl % 2]
                    ku = ["psB", "psD"][sl % 2]
                    for cc in range(NCC):
                        tk.op("pe", lambda t, psg=psg, gv=gv, cc=cc, c0=c0: t.matmul(
                              psg[:, :], gv[:, cc * 128:(cc + 1) * 128],
                              xnT[:, cc * T + c0:cc * T + c0 + 512],
                              start=(cc == 0), stop=(cc == NCC - 1)),
                              reads=[f"gch{p}", "xnT"], writes=[kg])
                        tk.op("pe", lambda t, psu=psu, uv=uv, cc=cc, c0=c0: t.matmul(
                              psu[:, :], uv[:, cc * 128:(cc + 1) * 128],
                              xnT[:, cc * T + c0:cc * T + c0 + 512],
                              start=(cc == 0), stop=(cc == NCC - 1)),
                              reads=[f"uch{p}", "xnT"], writes=[ku])
                    sgs = [sgv, vb(F_sg, 512)][0]
                    tk.op("act", lambda s, psg=psg, ic=ic, sl=sl: s.activation(
                          sg2[sl % 2], psg[:, :], AF.Silu, scale=gs_sb[:, ic:ic + 1]),
                          reads=[kg, "gs"], writes=[f"sg{sl % 2}"])
                    tk.op("dve", lambda v, psu=psu, k=k, c0=c0, ic=ic, sl=sl:
                          v.scalar_tensor_tensor(
                          vb(F_hdn + k * 2048 + c0, 512), psu[:, :], us_sb[:, ic:ic + 1],
                          sg2[sl % 2], AluOpType.mult, AluOpType.mult),
                          reads=[ku, "us", f"sg{sl % 2}"], writes=["hdn"])

            psE_sl = [psE[:, e * 512:(e + 1) * 512] for e in range(4)]
            tk.dma("qpool", vb(F_d[0], ICB * 128),
                   down[0, :, ic0 * 128:(ic0 + ICB) * 128], writes=["gch0", "gch1"])
            for ht in range(NCC):
                p = ht % 2
                dv = vb(F_d[p], ICB * 128)
                if ht + 1 < NCC:
                    tk.dma("qpool", vb(F_d[1 - p], ICB * 128),
                           down[ht + 1, :, ic0 * 128:(ic0 + ICB) * 128],
                           writes=(["uch0", "uch1"] if (ht + 1) % 2 else ["gch0", "gch1"]))
                for sl in range(4):
                    g = ht * 4 + sl
                    if g % 6 < 2:
                        ps, pskeys = [psC, psD][g % 6][:, :], [PSK[2 + g % 6]]
                    else:
                        ps, pskeys = psE_sl[g % 6 - 2], [EK[g % 6 - 2]]
                    c0 = sl * 512
                    pin = vf(F_pin[sl % 2], 512)
                    fo = vf(F_fo[sl % 2], 512)
                    tk.dma(["qsp", "qact"][sl % 2], pin, part_s[ht, :, c0:c0 + 512],
                           reads=[f"part{ht}_{sl}"], writes=[f"pin{sl % 2}"])
                    if r == 1:
                        xr = vf(F_xr[sl % 2], 512)
                        tk.dma(["qsp", "qact"][sl % 2], xr, xT[b, ht, :, c0:c0 + 512],
                               writes=[f"xr{sl % 2}"])
                    for k in range(ICB):
                        tk.op("pe", lambda t, ps=ps, dv=dv, k=k, c0=c0: t.matmul(ps,
                              dv[:, k * 128:(k + 1) * 128],
                              vb(F_hdn + k * 2048 + c0, 512),
                              start=(k == 0), stop=(k == ICB - 1)),
                              reads=(["uch0", "uch1"] if p else ["gch0", "gch1"]) + ["hdn"],
                              writes=pskeys)
                    tk.op("dve", lambda v, ps=ps, pin=pin, fo=fo, ht=ht, pskeys=pskeys:
                          v.scalar_tensor_tensor(fo, ps, ds_sb[:, ht:ht + 1], pin,
                          AluOpType.mult, AluOpType.add),
                          reads=pskeys + ["ds", f"pin{sl % 2}"], writes=[f"fo{sl % 2}"])
                    if r == 0:
                        tk.dma(["qsp", "qact"][sl % 2], part_s[ht, :, c0:c0 + 512], fo,
                               reads=[f"fo{sl % 2}"], writes=[f"part{ht}_{sl}"])
                    else:
                        xr = vf(F_xr[sl % 2], 512)
                        f2 = vf(F_f2[sl % 2], 512)
                        tk.op("dve", lambda v, fo=fo, xr=xr, f2=f2: v.tensor_tensor(
                              f2, fo, xr, AluOpType.add),
                              reads=[f"fo{sl % 2}", f"xr{sl % 2}"], writes=[f"f2{sl % 2}"])
                        tk.dma(["qsp", "qact"][sl % 2], outT[b, ht, :, c0:c0 + 512], f2,
                               reads=[f"f2{sl % 2}"])

        emit_init()
        for b in range(nbat):
            if "p0" in phases:
                emit_p0(b)
            tk.fence()
            if "attn" in phases:
                emit_ropetab()
                tk.fence()
                prefetch_head(0)
                for _ in s1_gen(b, 0):
                    pass
                fill = {"g": None, "blocked": False}

                def pump(n):
                    if fill["blocked"]:
                        return
                    g = fill["g"]
                    if g is None:
                        return
                    for _ in range(n):
                        try:
                            val = next(g)
                        except StopIteration:
                            fill["g"] = None
                            return
                        if val == "enzbar":
                            fill["blocked"] = True
                            return

                prev_s4b = None
                s4b_done = [True]

                def s4b_wrap(g, flag):
                    flag[0] = False
                    yield from g
                    flag[0] = True

                for h in range(nheads):
                    nxt = s1_gen(b, h + 1) if h + 1 < nheads else None
                    if prev_s4b is not None:
                        gens = [s4b_wrap(prev_s4b, s4b_done)]
                        if nxt is not None:
                            gens.append(nxt)
                        fill["g"] = itertools.chain(*gens)
                    else:
                        fill["g"] = nxt
                    dump = None
                    if b == 0 and h == 0 and dbg and "h0" in dbg:
                        v0 = hv(0)
                        dump = []
                        for nm, ap, key in (("dbg_QT", v0["QT"], v0["kQT"]),
                                            ("dbg_KT", v0["KT"], v0["kKT"]),
                                            ("dbg_Vtok", v0["Vtok"], v0["kVtok"]),
                                            ("dbg_sGT", v0["sGT"], v0["ksGT"]),
                                            ("dbg_es", v0["esv"], v0["kes"]),
                                            ("dbg_OG", vb(A_OG, 2048), "OG")):
                            dout(nm, [ap.shape[0], ap.shape[1]], BF16)
                            dump.append((nm, ap, [key]))
                        dout("dbg_enz", [128, 1024], F32)
                        dump.append(("dbg_enz", v0["enz"], [v0["kenz"]]))
                        dout("dbg_ss", [128, NTT], F32)
                        dump.append(("dbg_ss", ss_sb[:, :], ["ss"]))
                        dout("dbg_rstd", [128, NTT], F32)
                        dump.append(("dbg_rstd", rstd16[:, :], ["rstd16"]))
                    s2a(b, h, pump)
                    assert s4b_done[0], "s4b did not drain within s2a"
                    s2b(b, h, pump)
                    fill["blocked"] = False
                    s3(b, h, pump)
                    s4a(b, h, pump)
                    fill["blocked"] = False
                    pump(10 ** 9)
                    prev_s4b = s4b_gen(b, h, dump=dump)
                for _ in prev_s4b:
                    pass
                tk.fence()
            if "w" in phases:
                emit_w(b)
                tk.fence()
            if "ffn" in phases:
                emit_f(b, 0)
                emit_f(b, 1)
                tk.fence()
            if b == 0 and dbg:
                if "xnT" in dbg:
                    o = dout("dbg_xnT", [128, NCC * T], BF16)
                    tk.dma("qsp", o, xnT[:, :], reads=["xnT"])
                if "t1aug" in dbg:
                    o = dout("dbg_t1", [17, T], BF16)
                    tk.dma("qsp", o, t1aug[:, :], reads=["t1aug", "t1aug16"])
                if "ropetab" in dbg:
                    o1 = dout("dbg_cos", [128, T], BF16)
                    o2 = dout("dbg_sin", [128, T], BF16)
                    tk.dma("qsp", o1, vb(A_cos, 2048), reads=["cosT"])
                    tk.dma("qsp", o2, vb(A_sin, 2048), reads=["sinT"])
                if "part" in dbg:
                    o = dout("dbg_part", [NCC, 128, T], F32)
                    tk.dma("qsp", o, part_s,
                           reads=[f"part{ht}_{sl}" for ht in range(NCC) for sl in range(4)])
        tk.fence_all("sp")
        tk.finalize_incs()

        @block.gpsimd
        def _(g):
            tk.replay("pool", g)

        @block.tensor
        def _(t):
            tk.replay("pe", t)

        @block.vector
        def _(v):
            tk.replay("dve", v)

        @block.scalar
        def _(s):
            tk.replay("act", s)

        @block.sync
        def _(sp):
            tk.replay("sp", sp)

    return nc


def host_layout(inputs):
    x = np.ascontiguousarray(inputs["L0_attn_hidden_states"])
    Wq = inputs["L0_attn_Wq"]; Wk = inputs["L0_attn_Wk"]
    Wv = inputs["L0_attn_Wv"]; Wg = inputs["L0_attn_Wg"]
    Wo = inputs["L0_attn_Wo"]; Ws1 = inputs["L0_attn_Ws1"]
    Ws2 = inputs["L0_attn_Ws2"]; bs2 = inputs["L0_attn_bs2"]
    gnw = inputs["L0_attn_g_norm_weight"]
    gw = inputs["L0_ffn_gate_w"]; uw = inputs["L0_ffn_up_w"]; dw = inputs["L0_ffn_down_w"]
    gsc = inputs["L0_ffn_gate_s"]; usc = inputs["L0_ffn_up_s"]; dsc = inputs["L0_ffn_down_s"]

    def wT_heads(W):
        t = np.ascontiguousarray(W).T.reshape(NCC, 128, H, DK)
        return np.ascontiguousarray(t.transpose(2, 1, 0, 3).reshape(H, 128, NCC * DK))

    com = {
        "wq": wT_heads(Wq), "wk": wT_heads(Wk), "wv": wT_heads(Wv), "wg": wT_heads(Wg),
        "wo": np.ascontiguousarray(np.ascontiguousarray(Wo).T.reshape(H, 128, D)),
        "ws1t": np.ascontiguousarray(np.ascontiguousarray(Ws1).T.reshape(
            NCC, 128, 16).transpose(1, 0, 2).reshape(128, NCC * 16)),
        "ws2b": np.ascontiguousarray(np.concatenate(
            [np.ascontiguousarray(Ws2).T, np.asarray(bs2)[None, :]], axis=0)),
        "gnw": np.ascontiguousarray(np.asarray(gnw).reshape(128, 1)),
        "gate": np.ascontiguousarray(np.ascontiguousarray(gw).T.reshape(
            NCC, 128, NIC, 128).transpose(2, 1, 0, 3).reshape(NIC, 128, NCC * 128)),
        "upw": np.ascontiguousarray(np.ascontiguousarray(uw).T.reshape(
            NCC, 128, NIC, 128).transpose(2, 1, 0, 3).reshape(NIC, 128, NCC * 128)),
        "down": np.ascontiguousarray(np.ascontiguousarray(dw).T.reshape(
            NIC, 128, NCC, 128).transpose(2, 1, 0, 3).reshape(NCC, 128, NIC * 128)),
        "gs": np.ascontiguousarray(np.asarray(gsc).reshape(NIC, 128).T),
        "us": np.ascontiguousarray(np.asarray(usc).reshape(NIC, 128).T),
        "ds": np.ascontiguousarray(np.asarray(dsc).reshape(NCC, 128).T),
    }
    in_maps = []
    for core in range(8):
        m = dict(com)
        xb = x[2 * core:2 * core + 2]
        m["xT"] = np.ascontiguousarray(xb.transpose(0, 2, 1).reshape(NB, NCC, 128, T))
        in_maps.append(m)
    return in_maps


def kernel(**inputs):
    in_maps = host_layout(inputs)
    nc = build_program()
    res = run_bass_kernel_spmd(nc, in_maps, core_ids=list(range(8)))
    out = np.empty((16, T, D), dtype=np.float32)
    for core in range(8):
        oT = np.asarray(res.results[core]["outT"])
        for b in range(NB):
            out[2 * core + b] = oT[b].reshape(D, T).T
    return out
